# version 25
# baseline (speedup 1.0000x reference)
import numpy as np
import ml_dtypes
import concourse.bass as bass
import concourse.mybir as mybir
from concourse.bass_utils import run_bass_kernel_spmd
from contextlib import ExitStack

F32 = mybir.dt.float32
BF16 = mybir.dt.bfloat16
bf = ml_dtypes.bfloat16
AF = mybir.ActivationFunctionType
ALU = mybir.AluOpType
AX = mybir.AxisListType

ENGINES = ("sync", "tensor", "vector", "scalar", "gpsimd")


class Sem:
    __slots__ = ("name", "h", "count")

    def __init__(self, name, h):
        self.name, self.h, self.count = name, h, 0


class Buf:
    __slots__ = ("name", "w", "r")

    def __init__(self, name=""):
        self.name, self.w, self.r = name, None, []


class Plan:
    def __init__(self, nc, es):
        self.nc, self.es = nc, es
        self.tasks = {e: [] for e in ENGINES}
        self.sems = {e: self._mk("s_" + e) for e in ENGINES}
        self.DK = 8
        self.dsems = {e: [self._mk(f"d_{e}{k}") for k in range(self.DK)]
                      for e in ("sync", "scalar", "gpsimd")}
        self.ntasks = 0
        self.dhist = {}

    def _mk(self, name):
        return Sem(name, self.es.enter_context(self.nc.semaphore(name)))

    @staticmethod
    def _deps(reads, writes):
        deps = []
        for b in reads:
            if b.w is not None:
                deps.append(b.w)
        for b in writes:
            if b.w is not None:
                deps.append(b.w)
            deps.extend(b.r)
        return deps

    def _queue(self, eng, emit, sem, n, reads, writes, deps):
        alldeps = list(deps) + self._deps(reads, writes)
        sem.count += n
        tok = (sem, sem.count)
        self.tasks[eng].append((alldeps, emit, sem, n, self.ntasks))
        self.ntasks += 1
        for b in writes:
            b.w, b.r = tok, []
        for b in reads:
            b.r.append(tok)
        return tok

    def run(self, eng, emit, reads=(), writes=(), deps=()):
        return self._queue(eng, emit, self.sems[eng], 1, reads, writes, deps)

    def dma(self, eng, out, in_, reads=(), writes=(), deps=()):
        def emit(e, out=out, in_=in_):
            return e.dma_start(out=out, in_=in_)
        hist = self.dhist.setdefault(eng, [])
        i = len(hist)
        prev = hist[i - self.DK] if i >= self.DK else None
        alldeps = list(deps) + ([prev] if prev else [])
        tok = self._queue(eng, emit, self.dsems[eng][i % self.DK], 16,
                          reads, writes, alldeps)
        hist.append(tok)
        return tok

    def waiter(self, eng, deps):
        self.tasks[eng].append((list(deps), lambda e: None,
                                self.sems[eng], 0, self.ntasks))
        self.ntasks += 1

    def check(self):
        counts, idx = {}, {e: 0 for e in ENGINES}
        prog = True
        while prog:
            prog = False
            for e in ENGINES:
                while idx[e] < len(self.tasks[e]):
                    deps, _, sem, n, _t = self.tasks[e][idx[e]]
                    if all(counts.get(s.name, 0) >= v for (s, v) in deps):
                        counts[sem.name] = counts.get(sem.name, 0) + n
                        idx[e] += 1
                        prog = True
                    else:
                        break
        stuck = {e: i for e, i in idx.items() if i < len(self.tasks[e])}
        if stuck:
            msg = []
            for e, i in stuck.items():
                deps = self.tasks[e][i][0]
                blocking = [(s.name, v, counts.get(s.name, 0))
                            for (s, v) in deps if counts.get(s.name, 0) < v]
                msg.append(f"{e}[{i}/{len(self.tasks[e])}] blocked on {blocking}")
            raise RuntimeError("DEADLOCK in plan:\n" + "\n".join(msg))

    def emit(self, blk):
        self.check()

        import os as _os
        nodedup = bool(_os.environ.get("PLAN_NODEDUP"))

        def body_for(e):
            def body(eng):
                last = {}
                for deps, emitfn, sem, n, _t in self.tasks[e]:
                    need = {}
                    for (s, v) in deps:
                        if not nodedup and e == "tensor" and s is self.sems["tensor"]:
                            continue
                        if nodedup or v > last.get(s.name, 0):
                            cur = need.get(s.name)
                            need[s.name] = (s, v if cur is None else max(v, cur[1]))
                    for s, v in need.values():
                        eng.wait_ge(s.h, v)
                        last[s.name] = v
                    instr = emitfn(eng)
                    if n and instr is not None:
                        instr.then_inc(sem.h, n)
            return body

        for e in ENGINES:
            if self.tasks[e]:
                getattr(blk, e)(body_for(e))



D = 2048
T = 2048
NB = 2
HC = 16
NE = 8
LAMBDA_INIT = 0.8 - 0.6 * np.exp(-0.3 * 12)


def rope_perm_cols():
    p = np.zeros(D, dtype=np.int64)
    for c in range(D):
        h, r = divmod(c, 128)
        a, e = divmod(r, 64)
        Q, slot = divmod(e, 32)
        half, sub = divmod(slot, 16)[0], slot % 16
        i = Q * 16 + sub
        p[c] = (2 * h + a) * 64 + 2 * i + half
    return p


def rope_tables():
    cos = np.zeros((128, T), np.float32)
    sin = np.zeros((128, T), np.float32)
    t = np.arange(T, dtype=np.float64)
    for p in range(128):
        e = p % 64
        Q, slot = divmod(e, 32)
        half, sub = divmod(slot, 16)[0], slot % 16
        i = Q * 16 + sub
        invf = 10000.0 ** (-i / 32.0)
        f = t * invf
        cos[p] = np.cos(f)
        sin[p] = np.sin(f) * (1.0 if half == 1 else -1.0)
    return cos, sin


def causal_mask_tiles():
    m = np.zeros((128, 4, 512), np.float32)
    for j in range(4):
        for kl in range(128):
            m[kl, j, :min(128 * j + kl, 512)] = -1e30
    return m.reshape(128, 2048)



def build_program():
    nc = bass.Bass("TRN2", target_bir_lowering=False)

    dx = nc.dram_tensor("x", [NB, T, D], F32, kind="ExternalInput").ap()
    dwq = nc.dram_tensor("wq", [D, D], F32, kind="ExternalInput").ap()
    dwk = nc.dram_tensor("wk", [D, D], F32, kind="ExternalInput").ap()
    dwv = nc.dram_tensor("wv", [D, D], F32, kind="ExternalInput").ap()
    dwo = nc.dram_tensor("wo", [D, D], F32, kind="ExternalInput").ap()
    dfc1 = nc.dram_tensor("fc1", [NE, D, D], F32, kind="ExternalInput").ap()
    dfc2 = nc.dram_tensor("fc2", [NE, D, D], F32, kind="ExternalInput").ap()
    dgate = nc.dram_tensor("gate", [128, 128], F32, kind="ExternalInput").ap()
    db1 = nc.dram_tensor("b1r", [128, 128], F32, kind="ExternalInput").ap()
    db2 = nc.dram_tensor("b2", [8, D], F32, kind="ExternalInput").ap()
    dalpha = nc.dram_tensor("alphar", [128, 8], F32, kind="ExternalInput").ap()
    dlq1 = nc.dram_tensor("lq1", [1, 64], F32, kind="ExternalInput").ap()
    dlk1 = nc.dram_tensor("lk1", [1, 64], F32, kind="ExternalInput").ap()
    dlq2 = nc.dram_tensor("lq2", [1, 64], F32, kind="ExternalInput").ap()
    dlk2 = nc.dram_tensor("lk2", [1, 64], F32, kind="ExternalInput").ap()
    dlnw = nc.dram_tensor("lnw", [128, 1], F32, kind="ExternalInput").ap()
    dlnb = nc.dram_tensor("lnb", [128, 1], F32, kind="ExternalInput").ap()
    dcos = nc.dram_tensor("costab", [128, T], BF16, kind="ExternalInput").ap()
    dsin = nc.dram_tensor("sintab", [128, T], BF16, kind="ExternalInput").ap()
    dmask = nc.dram_tensor("masktab", [128, 2048], BF16, kind="ExternalInput").ap()
    dones = nc.dram_tensor("onestab", [128, 128], BF16, kind="ExternalInput").ap()
    dident = nc.dram_tensor("identtab", [128, 128], BF16, kind="ExternalInput").ap()
    dcst = nc.dram_tensor("csttab", [128, 8], F32, kind="ExternalInput").ap()
    dout = nc.dram_tensor("out", [NB, T, D], F32, kind="ExternalOutput").ap()
    swq = nc.dram_tensor("swq", [HC, 128, D], BF16, kind="Internal")
    swk = nc.dram_tensor("swk", [HC, 128, D], BF16, kind="Internal")
    swv = nc.dram_tensor("swv", [HC, 128, D], BF16, kind="Internal")
    swo = nc.dram_tensor("swo", [D, D], BF16, kind="Internal")
    sfc1 = nc.dram_tensor("sfc1", [NE, 4, 128, 8192], BF16, kind="Internal")
    sfc2 = nc.dram_tensor("sfc2", [NE, 4, 128, 8192], BF16, kind="Internal")
    sawo = nc.dram_tensor("sawo", [NB, 4, T, D], F32, kind="Internal")

    with ExitStack() as es:
        def sb(name, shape, dt=BF16):
            return es.enter_context(nc.sbuf_tensor(name, shape, dt))

        def psum(name, shape, dt=F32):
            return es.enter_context(nc.psum_tensor(name, shape, dt))

        s_cos = sb("s_cos", [128, T])
        s_sin = sb("s_sin", [128, T])
        s_mask = sb("s_mask", [128, 2048])
        s_ones = sb("s_ones", [128, 128])
        s_id = sb("s_id", [128, 128])
        s_cst = sb("s_cst", [128, 8], F32)
        s_gate = sb("s_gate", [128, 128], F32)
        s_gate_bf = sb("s_gate_bf", [128, 128])
        s_b1 = sb("s_b1", [128, 128], F32)
        s_b2bf = sb("s_b2bf", [8, D])
        s_alpha = sb("s_alpha", [128, 8], F32)
        s_lvec = sb("s_lvec", [1, 256], F32)
        s_lnw = sb("s_lnw", [128, 1], F32)
        s_lnb = sb("s_lnb", [128, 1], F32)
        s_lnwe = sb("s_lnwe", [128, 1], F32)
        s_lnbe = sb("s_lnbe", [128, 1], F32)
        s_stats = sb("s_stats", [128, 16], F32)
        s_gtmp = sb("s_gtmp", [128, 32], F32)
        s_scal = sb("s_scal", [1, 64], F32)
        s_bias = sb("s_bias", [128, 4], F32)
        s_bb = sb("s_bb", [1, 4])
        s_row = sb("s_row", [1, 2048], F32)
        s_rden = sb("s_rden", [1, 2048])
        s_cT = sb("s_cT", [8, 2048])
        s_call = sb("s_call", [128, 128], F32)
        s_cbf = sb("s_cbf", [128, 128])

        xn = sb("xn", [128, HC * T])
        ab = sb("ab", [128, 4 * T])
        yb = sb("yb", [128, 4 * T], F32)
        ybbf = yb[:].bitcast(BF16)
        az = sb("az", [128, 41 * 512])
        af = sb("af", [128, 9 * 512], F32)

        P = [psum(f"P{i}", [128, 512]) for i in range(7)]
        PT = psum("PT", [128, 1024], BF16)
        PTF = PT[:].bitcast(F32)

        p = Plan(nc, es)
        bC = Buf("consts")
        bXN = [Buf(f"xn{c}") for c in range(HC)]
        bAB = [Buf(f"ab{c}") for c in range(16)]
        bYB = [Buf(f"yb{c}") for c in range(4)]
        bAZ = [Buf(f"az{g}") for g in range(41)]
        bAF = [Buf(f"af{g}") for g in range(9)]
        bP = [Buf(f"P{i}") for i in range(7)]
        bPT = Buf("PT")
        bW = {n: Buf(n) for n in ("swq", "swk", "swv", "swo", "sfc1", "sfc2")}
        awo_toks = [[] for _ in range(NB)]
        bEV = [Buf(f"ev{j}") for j in range(4)]
        bSC = Buf("scal")
        bROW = Buf("rows")
        bRWp = [Buf("rw0"), Buf("rw1")]
        bBIAS = [Buf("bias0"), Buf("bias1")]
        bBB = [Buf("bb0"), Buf("bb1")]
        bSCp = [Buf("scal0"), Buf("scal1")]
        bST = Buf("stats")
        bGT = Buf("gatetmp")
        bLAM = Buf("lam")
        bC2 = Buf("cstuff")
        bOUT = Buf("outdma")

        def azs(g0, g1):
            return az[:, g0 * 512:g1 * 512], bAZ[g0:g1]

        def afs(g0, g1):
            return af[:, g0 * 512:g1 * 512], bAF[g0:g1]

        for dst, src in [(s_cos[:], dcos), (s_sin[:], dsin), (s_mask[:], dmask),
                         (s_ones[:], dones), (s_id[:], dident), (s_cst[:], dcst),
                         (s_gate[:], dgate), (s_b1[:], db1),
                         (s_alpha[:], dalpha), (s_lnw[:], dlnw), (s_lnb[:], dlnb)]:
            p.dma("sync", dst, src, writes=[bC])
        for i, src in enumerate([dlq1, dlk1, dlq2, dlk2]):
            p.dma("sync", s_lvec[:, 64 * i:64 * i + 64], src, writes=[bC])

        p.run("vector", lambda e: e.tensor_copy(s_gate_bf[:], s_gate[:]),
              reads=[bC], writes=[bC2])
        p.dma("gpsimd", s_b2bf[:], db2, writes=[bC2])
        p.run("vector", lambda e: e.tensor_scalar(s_lnwe[:], s_lnw[:],
                                                  float(1.0 - LAMBDA_INIT), None,
                                                  ALU.mult),
              reads=[bC], writes=[bC2])
        p.run("vector", lambda e: e.tensor_scalar(s_lnbe[:], s_lnb[:],
                                                  float(1.0 - LAMBDA_INIT), None,
                                                  ALU.mult),
              reads=[bC], writes=[bC2])
        p.run("vector", lambda e: e.tensor_mul(s_row[:, 0:64], s_lvec[:, 0:64],
                                               s_lvec[:, 64:128]),
              reads=[bC], writes=[bROW])
        p.run("vector", lambda e: e.reduce_sum(s_scal[:, 1:2], s_row[:, 0:64], axis=AX.X),
              reads=[bROW], writes=[bSC])
        p.run("vector", lambda e: e.tensor_mul(s_row[:, 0:64], s_lvec[:, 128:192],
                                               s_lvec[:, 192:256]),
              reads=[bC, bSC], writes=[bROW])
        p.run("vector", lambda e: e.reduce_sum(s_scal[:, 2:3], s_row[:, 0:64], axis=AX.X),
              reads=[bROW], writes=[bSC])
        p.run("scalar", lambda e: e.activation(s_scal[:, 3:4], s_scal[:, 1:2], AF.Exp),
              reads=[bSC], writes=[bSC])
        p.run("scalar", lambda e: e.activation(s_scal[:, 4:5], s_scal[:, 2:3], AF.Exp),
              reads=[bSC], writes=[bSC])
        p.run("vector", lambda e: e.tensor_sub(s_scal[:, 5:6], s_scal[:, 3:4],
                                               s_scal[:, 4:5]),
              reads=[bSC], writes=[bSC])
        p.run("vector", lambda e: e.tensor_scalar(s_scal[:, 0:1], s_scal[:, 5:6],
                                                  float(LAMBDA_INIT), None, ALU.add),
              reads=[bSC], writes=[bLAM])

        for w_img, w_src, buf in [(swq, dwq, bW["swq"]), (swk, dwk, bW["swk"]),
                                  (swv, dwv, bW["swv"])]:
            wv4 = w_src.rearrange("(kc p) (hc m) -> hc p kc m", p=128, m=128)
            for hcb in range(HC):
                p.dma("gpsimd",
                      w_img.ap()[hcb].rearrange("p (kc m) -> p kc m", m=128),
                      wv4[hcb], writes=[buf])
        for i in range(4):
            p.dma("gpsimd", swo.ap()[512 * i:512 * (i + 1), :],
                  dwo[512 * i:512 * (i + 1), :], writes=[bW["swo"]])
        for eidx in range(NE):
            for it in range(4):
                p.dma("gpsimd",
                      sfc1.ap()[eidx, it].rearrange("p (kc n) -> p kc n", n=512),
                      dfc1[eidx, :, 512 * it:512 * (it + 1)]
                      .rearrange("(kc p) n -> p kc n", p=128),
                      writes=[bW["sfc1"]])
                p.dma("gpsimd",
                      sfc2.ap()[eidx, it].rearrange("p (ic n) -> p ic n", n=512),
                      dfc2[eidx, :, 512 * it:512 * (it + 1)]
                      .rearrange("(ic p) n -> p ic n", p=128),
                      writes=[bW["sfc2"]])

        for b in range(NB):
            for tc in range(16):
                pr = tc % 2
                sb0 = 8 * pr
                xt, bxt = afs(4 * pr, 4 * pr + 4)
                sq, bsq = yb[:, 0:2048], [bYB[0]]
                xnt, bxnt = azs(32 + 4 * pr, 36 + 4 * pr)
                p.dma("sync", xt, dx[b, 128 * tc:128 * (tc + 1), :], writes=bxt)
                p.run("vector", lambda e, xt=xt, c=sb0: e.reduce_sum(
                    s_stats[:, c:c + 1], xt, axis=AX.X), reads=bxt, writes=[bST])
                p.run("vector", lambda e, xt=xt, sq=sq: e.tensor_mul(sq, xt, xt),
                      reads=bxt, writes=bsq)
                p.run("vector", lambda e, sq=sq, c=sb0: e.reduce_sum(
                    s_stats[:, c + 1:c + 2], sq, axis=AX.X), reads=bsq, writes=[bST])
                p.run("vector", lambda e, c=sb0: e.tensor_scalar(
                    s_stats[:, c + 2:c + 3], s_stats[:, c:c + 1], 1.0 / 2048, None,
                    ALU.mult), reads=[bST], writes=[bST])
                p.run("vector", lambda e, c=sb0: e.tensor_scalar(
                    s_stats[:, c + 3:c + 4], s_stats[:, c + 1:c + 2], 1.0 / 2048,
                    None, ALU.mult), reads=[bST], writes=[bST])
                p.run("vector", lambda e, c=sb0: e.tensor_scalar(
                    s_stats[:, c + 1:c + 2], s_stats[:, c + 2:c + 3],
                    s_stats[:, c + 2:c + 3], None, ALU.mult),
                    reads=[bST], writes=[bST])
                p.run("vector", lambda e, c=sb0: e.tensor_sub(
                    s_stats[:, c + 3:c + 4], s_stats[:, c + 3:c + 4],
                    s_stats[:, c + 1:c + 2]), reads=[bST], writes=[bST])
                p.run("scalar", lambda e, c=sb0: e.activation(
                    s_stats[:, c + 4:c + 5], s_stats[:, c + 3:c + 4], AF.Ln,
                    bias=s_cst[:, 0:1]), reads=[bST, bC], writes=[bST])
                p.run("scalar", lambda e, c=sb0: e.activation(
                    s_stats[:, c + 5:c + 6], s_stats[:, c + 4:c + 5], AF.Exp,
                    scale=-0.5), reads=[bST], writes=[bST])
                p.run("vector", lambda e, c=sb0: e.scalar_tensor_tensor(
                    s_stats[:, c + 6:c + 7], s_stats[:, c + 2:c + 3], -1.0,
                    s_stats[:, c + 5:c + 6], ALU.mult, ALU.mult),
                    reads=[bST], writes=[bST])
                p.run("scalar", lambda e, xt=xt, xnt=xnt, c=sb0: e.activation(
                    xnt, xt, AF.Identity, scale=s_stats[:, c + 5:c + 6],
                    bias=s_stats[:, c + 6:c + 7]),
                    reads=bxt + [bST], writes=bxnt)
                for half in range(2):
                    def tr(e, xnt=xnt, half=half):
                        last = None
                        for dcl in range(8):
                            dc = half * 8 + dcl
                            last = e.transpose(
                                PT[:, 128 * dcl:128 * (dcl + 1)],
                                xnt[:, 128 * dc:128 * (dc + 1)], s_id[:])
                        return last
                    p.run("tensor", tr, reads=bxnt + [bC], writes=[bPT])
                    out_ap = xn[:].rearrange("p (c n) -> p c n", c=16, n=2048)[
                        :, 8 * half:8 * half + 8, 128 * tc:128 * (tc + 1)]
                    in_ap = PT[:].rearrange("p (c n) -> p c n", c=8, n=128)
                    p.run("scalar", lambda e, out_ap=out_ap, in_ap=in_ap:
                          e.copy(out_ap, in_ap),
                          reads=[bPT], writes=[bXN[half * 8 + i] for i in range(8)])

            for tc in range(16):
                gps, bgps = (P[tc % 3], bP[tc % 3])

                def gmm(e, tc=tc, gps=gps):
                    last = None
                    for kc in range(16):
                        last = e.matmul(gps[0:128, 0:8],
                                        xn[:, 2048 * kc + 128 * tc:2048 * kc + 128 * (tc + 1)],
                                        s_gate_bf[:, 8 * kc:8 * (kc + 1)],
                                        start=(kc == 0), stop=(kc == 15))
                    return last
                p.run("tensor", gmm, reads=[bXN[i] for i in range(16)] + [bC2],
                      writes=[bgps])
                L = s_gtmp[:, 0:8]
                PEXP = s_gtmp[:, 8:16]
                MK2 = s_gtmp[:, 16:24]
                L2 = s_gtmp[:, 24:32]
                p.run("scalar", lambda e, L=L, gps=gps: e.copy(L, gps[0:128, 0:8]),
                      reads=[bgps], writes=[bGT])
                p.run("vector", lambda e, L=L: e.reduce_max(
                    s_stats[:, 0:1], L, axis=AX.X, negate=True),
                    reads=[bGT], writes=[bST])
                p.run("scalar", lambda e, L=L, PEXP=PEXP: e.activation(
                    PEXP, L, AF.Exp, bias=s_stats[:, 0:1]),
                    reads=[bGT, bST], writes=[bGT])
                p.run("vector", lambda e: e.tensor_scalar(
                    s_stats[:, 1:2], s_stats[:, 0:1], -1.0, None, ALU.mult),
                    reads=[bST], writes=[bST])
                p.run("vector", lambda e, L=L, L2=L2: e.tensor_scalar(
                    L2, L, s_stats[:, 1:2], None, ALU.is_ge),
                    reads=[bGT, bST], writes=[bGT])
                p.run("vector", lambda e, L=L, L2=L2: e.scalar_tensor_tensor(
                    L2, L2, -1e30, L, ALU.mult, ALU.add),
                    reads=[bGT], writes=[bGT])
                p.run("vector", lambda e, L2=L2: e.reduce_max(
                    s_stats[:, 2:3], L2, axis=AX.X),
                    reads=[bGT], writes=[bST])
                p.run("vector", lambda e, L=L, MK2=MK2: e.tensor_scalar(
                    MK2, L, s_stats[:, 2:3], None, ALU.is_ge),
                    reads=[bGT, bST], writes=[bGT])
                p.run("vector", lambda e, PEXP=PEXP, MK2=MK2: e.tensor_mul(
                    PEXP, PEXP, MK2), reads=[bGT], writes=[bGT])
                p.run("vector", lambda e, PEXP=PEXP: e.reduce_sum(
                    s_stats[:, 3:4], PEXP, axis=AX.X),
                    reads=[bGT], writes=[bST])
                p.run("vector", lambda e: e.reciprocal(
                    s_stats[:, 4:5], s_stats[:, 3:4]), reads=[bST], writes=[bST])
                p.run("vector", lambda e, PEXP=PEXP, tc=tc: e.scalar_tensor_tensor(
                    s_call[:, 8 * tc:8 * (tc + 1)], PEXP, s_stats[:, 4:5],
                    s_alpha[:, 0:8], ALU.mult, ALU.mult),
                    reads=[bGT, bST, bC], writes=[bC2])
            p.run("vector", lambda e: e.tensor_copy(s_cbf[:], s_call[:]),
                  reads=[bC2], writes=[bC2])
            for tc in range(16):
                p.run("tensor", lambda e, tc=tc: e.transpose(
                    PT[0:8, 0:128],
                    s_cbf[:, 8 * tc:8 * (tc + 1)], s_id[:]),
                    reads=[bC2, bC], writes=[bPT])
                p.run("scalar", lambda e, tc=tc: e.copy(
                    s_cT[:, 128 * tc:128 * (tc + 1)], PT[0:8, 0:128]),
                    reads=[bPT], writes=[bC2])

            WCB = [azs(4 * i, 4 * i + 4) for i in range(2)]
            QCH = [azs(8, 12), azs(12, 16)]
            KCH = [azs(16, 20), azs(20, 24)]
            VCH = [azs(24, 28), azs(28, 32)]
            SQS = azs(32, 36)
            RTM = [azs(32, 33), azs(33, 34), azs(34, 35)]
            PBS = [azs(36 + i, 37 + i) for i in range(4)]
            SQD = azs(40, 41)
            ROT = [(P[0], bP[0]), (P[1], bP[1]), (P[2], bP[2])]
            rr = {"n": 0}

            def nrot():
                ps, bps = ROT[rr["n"] % 3]
                rr["n"] += 1
                return ps, bps

            rq = {"n": 0}

            def nrot7():
                ps, bps = (P[rq["n"] % 7], bP[rq["n"] % 7])
                rq["n"] += 1
                return ps, bps

            gi = {"t": 0}
            defq = []
            DEFER = 20

            def tick(w=1):
                gi["t"] += w

            def pump(force=False):
                while defq and (force or defq[0][0] <= gi["t"]):
                    defq.pop(0)[1]()

            shmask = list(range(16, 32)) + list(range(16))

            def qkv_head(h, par):
                hb = h % 2
                qc, bqc = QCH[par]
                kc_, bkc = KCH[par]
                vc, bvc = VCH[par]
                bxnall = [bXN[i] for i in range(16)]
                wq_s, bwq = WCB[0]
                wk_s, bwk = WCB[1]
                p.dma("sync", wq_s, swq.ap()[h], reads=[bW["swq"]], writes=bwq)
                p.dma("sync", wk_s, swk.ap()[h], reads=[bW["swk"]], writes=bwk)
                for (wsl, bws, dst, bdst) in [(wq_s, bwq, qc, bqc),
                                              (wk_s, bwk, kc_, bkc)]:
                    for tt in range(4):
                        ps, bps = nrot7()

                        def mm(e, wsl=wsl, tt=tt, ps=ps):
                            last = None
                            for kk in range(16):
                                last = e.matmul(
                                    ps[:], wsl[:, 128 * kk:128 * (kk + 1)],
                                    xn[:, 2048 * kk + 512 * tt:2048 * kk + 512 * (tt + 1)],
                                    start=(kk == 0), stop=(kk == 15))
                            return last
                        p.run("tensor", mm, reads=bws + bxnall, writes=[bps])
                        tick(2)
                        pump()
                        raw, braw = RTM[0]
                        rot, brot = RTM[1]
                        t1, bt1 = RTM[2]
                        p.run("scalar", lambda e, raw=raw, ps=ps: e.copy(
                            raw[:, 0:512], ps[:]), reads=[bps], writes=braw)
                        p.run("vector", lambda e, raw=raw, rot=rot: e.stream_shuffle(
                            rot[:, 0:512], raw[:, 0:512], shmask),
                            reads=braw, writes=brot)
                        p.run("vector", lambda e, raw=raw, t1=t1, tt=tt: e.tensor_mul(
                            t1[:, 0:512], raw[:, 0:512],
                            s_cos[:, 512 * tt:512 * (tt + 1)]),
                            reads=braw + [bC], writes=bt1)
                        p.run("vector", lambda e, rot=rot, tt=tt: e.tensor_mul(
                            rot[:, 0:512], rot[:, 0:512],
                            s_sin[:, 512 * tt:512 * (tt + 1)]),
                            reads=brot + [bC], writes=brot)
                        p.run("vector", lambda e, dst=dst, t1=t1, rot=rot, tt=tt:
                              e.tensor_add(dst[:, 512 * tt:512 * (tt + 1)],
                                           t1[:, 0:512], rot[:, 0:512]),
                              reads=bt1 + brot, writes=bdst)
                wv_s, bwv = WCB[0]
                p.dma("sync", wv_s, swv.ap()[h], reads=[bW["swv"]], writes=bwv)
                for tcv in range(16):
                    ps, bps = nrot7()

                    def vmm(e, wv_s=wv_s, tcv=tcv, ps=ps):
                        last = None
                        for kk in range(16):
                            last = e.matmul(
                                ps[:, 0:128],
                                xn[:, 2048 * kk + 128 * tcv:2048 * kk + 128 * (tcv + 1)],
                                wv_s[:, 128 * kk:128 * (kk + 1)],
                                start=(kk == 0), stop=(kk == 15))
                        return last
                    p.run("tensor", vmm, reads=bwv + bxnall, writes=[bps])
                    tick(1)
                    pump()
                    p.run("scalar", lambda e, vc=vc, tcv=tcv, ps=ps: e.copy(
                        vc[:, 128 * tcv:128 * (tcv + 1)], ps[:, 0:128]),
                        reads=[bps], writes=bvc)
                sqs, bsqs = SQS
                sc0 = 32 * hb
                for (srctile, bsrc, cb) in [(qc, bqc, sc0 + 8), (kc_, bkc, sc0 + 16)]:
                    p.run("vector", lambda e, s=srctile, sqs=sqs: e.tensor_mul(
                        sqs, s, s), reads=bsrc, writes=bsqs)
                    for a in range(2):
                        for i4 in range(4):
                            ps, bps = nrot7()
                            p.run("tensor", lambda e, a=a, i4=i4, sqs=sqs, ps=ps:
                                  e.matmul(ps[0:1, 0:512],
                                           s_ones[64 * a:64 * (a + 1), 0:1],
                                           sqs[64 * a:64 * (a + 1),
                                               512 * i4:512 * (i4 + 1)],
                                           start=True, stop=True),
                                  reads=bsqs + [bC], writes=[bps])
                            p.run("vector", lambda e, ps=ps, c=cb + 4 * a + i4:
                                  e.reduce_max(s_scal[:, c:c + 1], ps[0:1, 0:512],
                                               axis=AX.X),
                                  reads=[bps], writes=[bSCp[hb]])
                for a in range(2):
                    q0 = sc0 + 8 + 4 * a
                    k0 = sc0 + 16 + 4 * a
                    t0 = sc0 + 24
                    p.run("vector", lambda e, q0=q0, t0=t0: e.reduce_max(
                        s_scal[:, t0:t0 + 1], s_scal[:, q0:q0 + 4], axis=AX.X),
                        reads=[bSCp[hb]], writes=[bSCp[hb]])
                    p.run("vector", lambda e, k0=k0, t0=t0: e.reduce_max(
                        s_scal[:, t0 + 1:t0 + 2], s_scal[:, k0:k0 + 4], axis=AX.X),
                        reads=[bSCp[hb]], writes=[bSCp[hb]])
                    p.run("vector", lambda e, t0=t0: e.tensor_mul(
                        s_scal[:, t0 + 2:t0 + 3], s_scal[:, t0:t0 + 1],
                        s_scal[:, t0 + 1:t0 + 2]), reads=[bSCp[hb]], writes=[bSCp[hb]])
                    p.run("scalar", lambda e, t0=t0: e.activation(
                        s_scal[:, t0 + 3:t0 + 4], s_scal[:, t0 + 2:t0 + 3], AF.Ln),
                        reads=[bSCp[hb]], writes=[bSCp[hb]])
                    p.run("scalar", lambda e, t0=t0: e.activation(
                        s_scal[:, t0 + 4:t0 + 5], s_scal[:, t0 + 3:t0 + 4], AF.Exp,
                        scale=0.5), reads=[bSCp[hb]], writes=[bSCp[hb]])
                    p.run("vector", lambda e, t0=t0: e.tensor_scalar(
                        s_scal[:, t0 + 5:t0 + 6], s_scal[:, t0 + 4:t0 + 5], -0.125,
                        None, ALU.mult), reads=[bSCp[hb]], writes=[bSCp[hb]])
                    p.run("vector", lambda e, t0=t0, c=2 * hb + a: e.tensor_copy(
                        s_bb[0:1, c:c + 1], s_scal[:, t0 + 5:t0 + 6]),
                        reads=[bSCp[hb]], writes=[bBB[hb]])
                    ps, bps = nrot7()
                    p.run("tensor", lambda e, c=2 * hb + a, ps=ps: e.matmul(
                        ps[:, 0:1], s_ones[0:1, :], s_bb[0:1, c:c + 1],
                        start=True, stop=True), reads=[bBB[hb], bC], writes=[bps])
                    p.run("scalar", lambda e, c=2 * hb + a, ps=ps: e.copy(
                        s_bias[:, c:c + 1], ps[:, 0:1]), reads=[bps],
                        writes=[bBIAS[hb]])

            def attn_head(h, par, hslot):
                qc, bqc = QCH[par]
                kc_, bkc = KCH[par]
                vc, bvc = VCH[par]
                hb = h % 2
                LAG = 3
                insts = [(qt, kk, a) for qt in range(4)
                         for kk in range(4 * qt + 4) for a in range(2)]
                lastof = {}
                for i, (qt, kk, a) in enumerate(insts):
                    lastof[qt] = i
                sbank = {}

                def emit_dpv(idx):
                    qt, kk, a = insts[idx]
                    nkc = 4 * qt + 4
                    pu, bpu = (P[3], bP[3]) if a == 0 else (P[4], bP[4])
                    pd, bpd = (P[5], bP[5]) if a == 0 else (P[6], bP[6])
                    pbuf, bpb = PBS[idx % 4]

                    def dpv(e, kk=kk, nkc=nkc, pu=pu, pd=pd, pbuf=pbuf, vc=vc):
                        e.matmul(pd[0:1, :], s_ones[:, 0:1], pbuf[:, 0:512],
                                 start=(kk == 0), stop=(kk == nkc - 1))
                        return e.matmul(pu[:], vc[:, 128 * kk:128 * (kk + 1)],
                                        pbuf[:, 0:512],
                                        start=(kk == 0), stop=(kk == nkc - 1))
                    p.run("tensor", dpv, reads=bpb + bvc + [bC],
                          writes=[bpu, bpd])

                def emit_combine_a(qt):
                    p_ = qt % 2
                    r0 = 1024 * p_
                    U1C, bU1C = afs(5 + 2 * p_, 6 + 2 * p_)
                    U2C, bU2C = afs(6 + 2 * p_, 7 + 2 * p_)
                    p.run("vector", lambda e, r0=r0: e.reciprocal(
                        s_row[:, r0:r0 + 512], P[5][0:1, :]),
                        reads=[bP[5]], writes=[bRWp[p_]])
                    p.run("vector", lambda e, r0=r0: e.reciprocal(
                        s_row[:, r0 + 512:r0 + 1024], P[6][0:1, :]),
                        reads=[bP[6]], writes=[bRWp[p_]])
                    p.run("vector", lambda e, U1C=U1C: e.tensor_copy(
                        U1C, P[3][:]), reads=[bP[3]], writes=bU1C)
                    p.run("vector", lambda e, U2C=U2C: e.tensor_copy(
                        U2C, P[4][:]), reads=[bP[4]], writes=bU2C)

                def cb_part1(qt):
                    p_ = qt % 2
                    r0 = 1024 * p_
                    RR1, bRR1 = afs(2, 3)
                    p.run("vector", lambda e, r0=r0: e.tensor_copy(
                        s_rden[0:1, r0:r0 + 512], s_row[:, r0:r0 + 512]),
                        reads=[bRWp[p_]], writes=[bRWp[p_]])
                    p.run("vector", lambda e, r0=r0: e.tensor_scalar(
                        s_rden[0:1, r0 + 512:r0 + 1024],
                        s_row[:, r0 + 512:r0 + 1024],
                        s_scal[:, 0:1], None, ALU.mult),
                        reads=[bRWp[p_], bLAM], writes=[bRWp[p_]])
                    p.run("tensor", lambda e, r0=r0: e.matmul(
                        PTF, s_ones[0:1, :], s_rden[0:1, r0:r0 + 512],
                        start=True, stop=True), reads=[bRWp[p_], bC], writes=[bPT])
                    p.run("scalar", lambda e, RR1=RR1: e.copy(RR1, PTF),
                          reads=[bPT], writes=bRR1)

                def cb_part2(qt):
                    p_ = qt % 2
                    r0 = 1024 * p_
                    U1C, bU1C = afs(5 + 2 * p_, 6 + 2 * p_)
                    U2C, bU2C = afs(6 + 2 * p_, 7 + 2 * p_)
                    DIF, bDIF = afs(0, 1)
                    A2S, bA2S = afs(1, 2)
                    RR1, bRR1 = afs(2, 3)
                    RR2, bRR2 = afs(3, 4)
                    sqd, bsqd = SQD
                    p.run("tensor", lambda e, r0=r0: e.matmul(
                        PTF, s_ones[0:1, :], s_rden[0:1, r0 + 512:r0 + 1024],
                        start=True, stop=True), reads=[bRWp[p_], bC], writes=[bPT])
                    p.run("scalar", lambda e, RR2=RR2: e.copy(RR2, PTF),
                          reads=[bPT], writes=bRR2)
                    p.run("vector", lambda e, DIF=DIF, RR1=RR1, U1C=U1C:
                          e.tensor_mul(DIF, U1C, RR1),
                          reads=bU1C + bRR1, writes=bDIF)
                    p.run("vector", lambda e, A2S=A2S, RR2=RR2, U2C=U2C:
                          e.tensor_mul(A2S, U2C, RR2),
                          reads=bU2C + bRR2, writes=bA2S)
                    p.run("vector", lambda e, DIF=DIF, A2S=A2S:
                          e.tensor_sub(DIF, DIF, A2S),
                          reads=bA2S + bDIF, writes=bDIF)
                    p.run("vector", lambda e, DIF=DIF, sqd=sqd: e.tensor_mul(
                        sqd[:, 0:512], DIF, DIF), reads=bDIF, writes=bsqd)

                def cb_part3(qt):
                    p_ = qt % 2
                    r0 = 1024 * p_
                    sqd, bsqd = SQD
                    p.run("tensor", lambda e, sqd=sqd: e.matmul(
                        PTF[0:1, 0:512], s_ones[:, 0:1], sqd[:, 0:512],
                        start=True, stop=True), reads=bsqd + [bC], writes=[bPT])
                    p.run("scalar", lambda e, r0=r0: e.activation(
                        s_row[:, r0:r0 + 512], PTF[0:1, 0:512], AF.Ln,
                        scale=1.0 / 128, bias=s_cst[0:1, 1:2]),
                        reads=[bPT, bC], writes=[bRWp[p_]])
                    p.run("scalar", lambda e, r0=r0: e.activation(
                        s_row[:, r0:r0 + 512], s_row[:, r0:r0 + 512], AF.Exp,
                        scale=-0.5), reads=[bRWp[p_]], writes=[bRWp[p_]])
                    p.run("vector", lambda e, r0=r0: e.tensor_copy(
                        s_rden[0:1, r0:r0 + 512], s_row[:, r0:r0 + 512]),
                        reads=[bRWp[p_]], writes=[bRWp[p_]])

                def cb_part4(qt, hslot=hslot):
                    p_ = qt % 2
                    r0 = 1024 * p_
                    DIF, bDIF = afs(0, 1)
                    RMR, bRMR = afs(4, 5)
                    p.run("tensor", lambda e, r0=r0: e.matmul(
                        PTF, s_ones[0:1, :], s_rden[0:1, r0:r0 + 512],
                        start=True, stop=True), reads=[bRWp[p_], bC], writes=[bPT])
                    p.run("scalar", lambda e, RMR=RMR: e.copy(RMR, PTF),
                          reads=[bPT], writes=bRMR)
                    p.run("vector", lambda e, DIF=DIF, RMR=RMR: e.tensor_mul(
                        DIF, DIF, RMR), reads=bDIF + bRMR, writes=bDIF)
                    p.run("scalar", lambda e, DIF=DIF, qt=qt:
                          e.activation(ab[:, 2048 * hslot + 512 * qt:
                                          2048 * hslot + 512 * (qt + 1)],
                                       DIF, AF.Identity, scale=s_lnwe[:],
                                       bias=s_lnbe[:]),
                          reads=bDIF + [bC2], writes=[bAB[4 * hslot + qt]])

                def post_s(i):
                    if LAG <= i < len(insts) + LAG:
                        emit_dpv(i - LAG)
                        pqt = insts[i - LAG][0]
                        if i - LAG == lastof[pqt]:
                            emit_combine_a(pqt)
                            cb_part1(pqt)
                            cb_part2(pqt)
                            cb_part3(pqt)
                            cb_part4(pqt)

                for i, (qt, kk, a) in enumerate(insts):
                    j = kk - 4 * qt
                    ps, bps = nrot()
                    sbank[i] = (ps, bps)

                    def sg(e, a=a, kk=kk, qt=qt, j=j, ps=ps, kc_=kc_, qc=qc):
                        last = e.matmul(ps[:], kc_[64 * a:64 * (a + 1),
                                                   128 * kk:128 * (kk + 1)],
                                        qc[64 * a:64 * (a + 1),
                                           512 * qt:512 * (qt + 1)],
                                        start=True, stop=(j < 0))
                        if j >= 0:
                            last = e.matmul(ps[:], s_id[:],
                                            s_mask[:, 512 * j:512 * (j + 1)],
                                            start=False, stop=True)
                        return last
                    p.run("tensor", sg, reads=bkc + bqc + [bC], writes=[bps])
                    pbuf, bpb = PBS[i % 4]
                    p.run("scalar", lambda e, ps=ps, pbuf=pbuf, c=2 * hb + a:
                          e.activation(pbuf[:, 0:512], ps[:], AF.Exp,
                                       scale=0.125, bias=s_bias[:, c:c + 1]),
                          reads=[bps, bBIAS[hb]], writes=bpb)
                    post_s(i)
                    tick(1)
                    pump()
                n = len(insts)
                for i in range(n, n + LAG + 1):
                    post_s(i)

            WOS = ybbf[:, 0:8192]

            def wo_load(hp):
                for i in range(4):
                    p.dma("sync", WOS[:, 2048 * i:2048 * (i + 1)],
                          swo.ap()[128 * (4 * hp + i):128 * (4 * hp + i + 1), :],
                          reads=[bW["swo"]], writes=[bYB[i // 2]])

            def wo_pass(hp):
                for qc in range(16):
                    if defq:
                        defq.pop(0)[1]()
                    for dt in range(4):
                        ppar = (4 * qc + dt) % 4
                        ps, bps = (P[ppar], bP[ppar])

                        def womm(e, qc=qc, dt=dt, ps=ps):
                            last = None
                            for i in range(4):
                                last = e.matmul(
                                    ps[:], ab[:, 2048 * i + 128 * qc:
                                              2048 * i + 128 * (qc + 1)],
                                    WOS[:, 2048 * i + 512 * dt:
                                        2048 * i + 512 * (dt + 1)],
                                    start=(i == 0), stop=(i == 3))
                            return last
                        p.run("tensor", womm,
                              reads=[bAB[4 * i + qc // 4] for i in range(4)]
                              + [bYB[0], bYB[1]],
                              writes=[bps])
                        if qc == 0 and dt == 0:
                            assert len(defq) <= 7, len(defq)
                        ej = (4 * qc + dt) % 4
                        EV = yb[:, 4096 + 512 * ej:4608 + 512 * ej]
                        p.run("vector", lambda e, EV=EV, ps=ps: e.tensor_copy(
                            EV, ps[:]), reads=[bps], writes=[bEV[ej]])
                        awo_toks[b].append(p.dma(
                            "gpsimd",
                            sawo.ap()[b, hp, 128 * qc:128 * (qc + 1),
                                      512 * dt:512 * (dt + 1)],
                            EV, reads=[bEV[ej]]))
                if hp < 3:
                    wo_load(hp + 1)

            for hp in range(4):
                heads = list(range(4 * hp, 4 * hp + 4))
                if hp == 0:
                    wo_load(0)
                    qkv_head(heads[0], 0)
                for i, h in enumerate(heads):
                    if i + 1 < len(heads):
                        qkv_head(heads[i + 1], (i + 1) % 2)
                    elif hp < 3:
                        qkv_head(4 * hp + 4, 0)
                    attn_head(h, i % 2, h - 4 * hp)
                wo_pass(hp)

            FCR = [azs(16 * i, 16 * (i + 1)) for i in range(2)]
            STG = [afs(0, 4), afs(4, 8)]

            def emit_b2(tt):
                for tc in range(4):
                    gtc = 4 * tt + tc
                    ysl = yb[:, 2048 * tc:2048 * (tc + 1)]
                    for ht in range(4):
                        bi = [2, 5, 6][(4 * tc + ht) % 3]
                        p.run("tensor", lambda e, gtc=gtc, ht=ht, bi=bi: e.matmul(
                            P[bi][:], s_cT[:, 128 * gtc:128 * (gtc + 1)],
                            s_b2bf[:, 512 * ht:512 * (ht + 1)],
                            start=True, stop=True),
                            reads=[bC2], writes=[bP[bi]])
                        ywr = [bYB[tc]] + (bEV if (tt == 0 and tc == 2) else [])
                        p.run("vector", lambda e, ysl=ysl, ht=ht, bi=bi:
                              e.tensor_copy(ysl[:, 512 * ht:512 * (ht + 1)],
                                            P[bi][:]),
                              reads=[bP[bi]], writes=ywr)

            emit_b2(0)
            for tt in range(4):
                sct = {"n": 0}
                for tc in range(4):
                    gtc = 4 * tt + tc
                    ysl = yb[:, 2048 * tc:2048 * (tc + 1)]
                    srcs = [dx[b, 128 * gtc:128 * (gtc + 1), :]] + [
                        sawo.ap()[b, hp, 128 * gtc:128 * (gtc + 1), :]
                        for hp in range(4)]
                    for src in srcs:
                        stg, bstg = STG[sct["n"] % 2]
                        sct["n"] += 1
                        p.dma("gpsimd", stg[:, 0:2048], src,
                              writes=bstg, deps=awo_toks[b])
                        p.run("vector", lambda e, ysl=ysl, stg=stg: e.tensor_add(
                            ysl, ysl, stg[:, 0:2048]),
                            reads=bstg + [bYB[tc]], writes=[bYB[tc]])
                for eidx in range(NE):
                    hsl = ab[:]
                    bhsl = list(bAB)
                    for it in range(4):
                        ring, bring = FCR[(eidx * 8 + it) % 2]
                        p.dma("sync", ring, sfc1.ap()[eidx, it],
                              reads=[bW["sfc1"]], writes=bring)
                        for lic in range(4):
                            ic = 4 * it + lic
                            ppar = ic % 2
                            ps, bps = (P[ppar], bP[ppar])

                            def f1mm(e, ring=ring, lic=lic, tt=tt, ps=ps):
                                last = None
                                for kk in range(16):
                                    last = e.matmul(
                                        ps[:],
                                        ring[:, 512 * kk + 128 * lic:
                                             512 * kk + 128 * (lic + 1)],
                                        xn[:, 2048 * kk + 512 * tt:
                                           2048 * kk + 512 * (tt + 1)],
                                        start=(kk == 0), stop=(kk == 15))
                                return last
                            p.run("tensor", f1mm,
                                  reads=bring + [bXN[i] for i in range(16)],
                                  writes=[bps])
                            p.run("scalar", lambda e, hsl=hsl, ic=ic, ps=ps,
                                  eidx=eidx: e.activation(
                                      hsl[:, 512 * ic:512 * (ic + 1)], ps[:],
                                      AF.Gelu, bias=s_b1[:, 16 * eidx + ic:
                                                         16 * eidx + ic + 1]),
                                  reads=[bps, bC], writes=[bAB[ic]])
                    for ht in range(4):
                        ring, bring = FCR[(eidx * 8 + 4 + ht) % 2]
                        p.dma("sync", ring, sfc2.ap()[eidx, ht],
                              reads=[bW["sfc2"]], writes=bring)
                        for tc in range(4):
                            gtc = 4 * tt + tc
                            ppar = (4 * ht + tc) % 2
                            ps, bps = (P[3 + ppar], bP[3 + ppar])

                            def f2mm(e, ring=ring, hsl=hsl, tc=tc, ps=ps):
                                last = None
                                for ic in range(16):
                                    last = e.matmul(
                                        ps[:],
                                        hsl[:, 512 * ic + 128 * tc:
                                            512 * ic + 128 * (tc + 1)],
                                        ring[:, 512 * ic:512 * (ic + 1)],
                                        start=(ic == 0), stop=(ic == 15))
                                return last
                            p.run("tensor", f2mm, reads=bring + bhsl,
                                  writes=[bps])
                            ysl = yb[:, 2048 * tc + 512 * ht:
                                     2048 * tc + 512 * (ht + 1)]
                            p.run("vector", lambda e, ysl=ysl, ps=ps, gtc=gtc,
                                  eidx=eidx: e.scalar_tensor_tensor(
                                      ysl, ps[:],
                                      s_call[:, 8 * gtc + eidx:8 * gtc + eidx + 1],
                                      ysl, ALU.mult, ALU.add),
                                  reads=[bps, bC2, bYB[tc]], writes=[bYB[tc]])
                for tc in range(4):
                    gtc = 4 * tt + tc
                    p.dma("scalar", dout[b, 128 * gtc:128 * (gtc + 1), :],
                          yb[:, 2048 * tc:2048 * (tc + 1)],
                          reads=[bYB[tc]])
                if tt < 3:
                    emit_b2(tt + 1)

        p.waiter("scalar", [(s, s.count) for s in p.dsems["scalar"]])

        with nc.allow_low_precision(reason="bf16 matmul operands by design"):
            with nc.Block() as blk:
                p.emit(blk)
    return nc



_NC_CACHE = {}


def kernel(L0_attn_x, L0_attn_Wq, L0_attn_Wk, L0_attn_Wv, L0_attn_Wo,
           L0_attn_lambda_q1, L0_attn_lambda_k1, L0_attn_lambda_q2,
           L0_attn_lambda_k2, L0_attn_ln_w, L0_attn_ln_b, L0_ffn_gate_w,
           L0_ffn_fc1_w, L0_ffn_fc1_b, L0_ffn_fc2_w, L0_ffn_fc2_b,
           L0_ffn_alpha):
    x = np.ascontiguousarray(np.asarray(L0_attn_x, dtype=np.float32))
    perm = rope_perm_cols()
    wq = np.ascontiguousarray(np.asarray(L0_attn_Wq, np.float32)[:, perm])
    wk = np.ascontiguousarray(np.asarray(L0_attn_Wk, np.float32)[:, perm])
    wv = np.ascontiguousarray(np.asarray(L0_attn_Wv, np.float32))
    wo = np.ascontiguousarray(np.asarray(L0_attn_Wo, np.float32))
    fc1 = np.ascontiguousarray(np.asarray(L0_ffn_fc1_w, np.float32))
    fc2 = np.ascontiguousarray(np.asarray(L0_ffn_fc2_w, np.float32))
    gate = np.ascontiguousarray(
        np.asarray(L0_ffn_gate_w, np.float32).reshape(16, 128, 8)
        .transpose(1, 0, 2).reshape(128, 128))
    b1r = np.ascontiguousarray(
        np.asarray(L0_ffn_fc1_b, np.float32).reshape(8, 16, 128)
        .transpose(2, 0, 1).reshape(128, 128))
    b2 = np.ascontiguousarray(np.asarray(L0_ffn_fc2_b, np.float32))
    alphar = np.ascontiguousarray(
        np.tile(np.asarray(L0_ffn_alpha, np.float32)[None, :], (128, 1)))
    cos_t, sin_t = rope_tables()
    consts = {
        "costab": cos_t.astype(bf), "sintab": sin_t.astype(bf),
        "masktab": causal_mask_tiles().astype(bf),
        "onestab": np.ones((128, 128), np.float32).astype(bf),
        "identtab": np.eye(128, dtype=np.float32).astype(bf),
        "csttab": np.tile(np.array([[1e-5, 1e-8, 0, 0, 0, 0, 0, 0]],
                                   np.float32), (128, 1)),
    }
    smalls = {
        "wq": wq, "wk": wk, "wv": wv, "wo": wo, "fc1": fc1, "fc2": fc2,
        "gate": gate, "b1r": b1r, "b2": b2, "alphar": alphar,
        "lq1": np.asarray(L0_attn_lambda_q1, np.float32).reshape(1, 64),
        "lk1": np.asarray(L0_attn_lambda_k1, np.float32).reshape(1, 64),
        "lq2": np.asarray(L0_attn_lambda_q2, np.float32).reshape(1, 64),
        "lk2": np.asarray(L0_attn_lambda_k2, np.float32).reshape(1, 64),
        "lnw": np.asarray(L0_attn_ln_w, np.float32).reshape(128, 1),
        "lnb": np.asarray(L0_attn_ln_b, np.float32).reshape(128, 1),
    }
    in_maps = []
    for core in range(8):
        m = {"x": np.ascontiguousarray(x[2 * core:2 * core + 2])}
        m.update(smalls)
        m.update(consts)
        in_maps.append(m)
    if "nc" not in _NC_CACHE:
        _NC_CACHE["nc"] = build_program()
    nc = _NC_CACHE["nc"]
    res = run_bass_kernel_spmd(nc, in_maps, core_ids=list(range(8)))
    out = np.concatenate([np.asarray(r["out"], np.float32)
                          for r in res.results], axis=0)
    return out


# version 26
# speedup vs baseline: 1.0622x; 1.0622x over previous
import numpy as np
import ml_dtypes
import concourse.bass as bass
import concourse.mybir as mybir
from concourse.bass_utils import run_bass_kernel_spmd
from contextlib import ExitStack

F32 = mybir.dt.float32
BF16 = mybir.dt.bfloat16
bf = ml_dtypes.bfloat16
AF = mybir.ActivationFunctionType
ALU = mybir.AluOpType
AX = mybir.AxisListType

ENGINES = ("sync", "tensor", "vector", "scalar", "gpsimd")


class Sem:
    __slots__ = ("name", "h", "count")

    def __init__(self, name, h):
        self.name, self.h, self.count = name, h, 0


class Buf:
    __slots__ = ("name", "w", "r")

    def __init__(self, name=""):
        self.name, self.w, self.r = name, None, []


class Plan:
    def __init__(self, nc, es):
        self.nc, self.es = nc, es
        self.tasks = {e: [] for e in ENGINES}
        self.sems = {e: self._mk("s_" + e) for e in ENGINES}
        self.DK = 4
        self.dsems = {e: [self._mk(f"d_{e}{k}") for k in range(self.DK)]
                      for e in ("sync", "scalar", "gpsimd")}
        self.ntasks = 0
        self.dhist = {}

    def _mk(self, name):
        return Sem(name, self.es.enter_context(self.nc.semaphore(name)))

    @staticmethod
    def _deps(reads, writes):
        deps = []
        for b in reads:
            if b.w is not None:
                deps.append(b.w)
        for b in writes:
            if b.w is not None:
                deps.append(b.w)
            deps.extend(b.r)
        return deps

    def _queue(self, eng, emit, sem, n, reads, writes, deps):
        alldeps = list(deps) + self._deps(reads, writes)
        sem.count += n
        tok = (sem, sem.count)
        self.tasks[eng].append((alldeps, emit, sem, n, self.ntasks))
        self.ntasks += 1
        for b in writes:
            b.w, b.r = tok, []
        for b in reads:
            b.r.append(tok)
        return tok

    def run(self, eng, emit, reads=(), writes=(), deps=()):
        return self._queue(eng, emit, self.sems[eng], 1, reads, writes, deps)

    def dma(self, eng, out, in_, reads=(), writes=(), deps=()):
        def emit(e, out=out, in_=in_):
            return e.dma_start(out=out, in_=in_)
        hist = self.dhist.setdefault(eng, [])
        i = len(hist)
        prev = hist[i - self.DK] if i >= self.DK else None
        alldeps = list(deps) + ([prev] if prev else [])
        tok = self._queue(eng, emit, self.dsems[eng][i % self.DK], 16,
                          reads, writes, alldeps)
        hist.append(tok)
        return tok

    def waiter(self, eng, deps):
        self.tasks[eng].append((list(deps), lambda e: None,
                                self.sems[eng], 0, self.ntasks))
        self.ntasks += 1

    def check(self):
        counts, idx = {}, {e: 0 for e in ENGINES}
        prog = True
        while prog:
            prog = False
            for e in ENGINES:
                while idx[e] < len(self.tasks[e]):
                    deps, _, sem, n, _t = self.tasks[e][idx[e]]
                    if all(counts.get(s.name, 0) >= v for (s, v) in deps):
                        counts[sem.name] = counts.get(sem.name, 0) + n
                        idx[e] += 1
                        prog = True
                    else:
                        break
        stuck = {e: i for e, i in idx.items() if i < len(self.tasks[e])}
        if stuck:
            msg = []
            for e, i in stuck.items():
                deps = self.tasks[e][i][0]
                blocking = [(s.name, v, counts.get(s.name, 0))
                            for (s, v) in deps if counts.get(s.name, 0) < v]
                msg.append(f"{e}[{i}/{len(self.tasks[e])}] blocked on {blocking}")
            raise RuntimeError("DEADLOCK in plan:\n" + "\n".join(msg))

    def emit(self, blk):
        self.check()

        import os as _os
        nodedup = bool(_os.environ.get("PLAN_NODEDUP"))

        def body_for(e):
            def body(eng):
                last = {}
                for deps, emitfn, sem, n, _t in self.tasks[e]:
                    need = {}
                    for (s, v) in deps:
                        if not nodedup and e == "tensor" and s is self.sems["tensor"]:
                            continue
                        if nodedup or v > last.get(s.name, 0):
                            cur = need.get(s.name)
                            need[s.name] = (s, v if cur is None else max(v, cur[1]))
                    for s, v in need.values():
                        eng.wait_ge(s.h, v)
                        last[s.name] = v
                    instr = emitfn(eng)
                    if n and instr is not None:
                        instr.then_inc(sem.h, n)
            return body

        for e in ENGINES:
            if self.tasks[e]:
                getattr(blk, e)(body_for(e))



D = 2048
T = 2048
NB = 2
HC = 16
NE = 8
LAMBDA_INIT = 0.8 - 0.6 * np.exp(-0.3 * 12)


def rope_perm_cols():
    p = np.zeros(D, dtype=np.int64)
    for c in range(D):
        h, r = divmod(c, 128)
        a, e = divmod(r, 64)
        Q, slot = divmod(e, 32)
        half, sub = divmod(slot, 16)[0], slot % 16
        i = Q * 16 + sub
        p[c] = (2 * h + a) * 64 + 2 * i + half
    return p


def rope_tables():
    cos = np.zeros((128, T), np.float32)
    sin = np.zeros((128, T), np.float32)
    t = np.arange(T, dtype=np.float64)
    for p in range(128):
        e = p % 64
        Q, slot = divmod(e, 32)
        half, sub = divmod(slot, 16)[0], slot % 16
        i = Q * 16 + sub
        invf = 10000.0 ** (-i / 32.0)
        f = t * invf
        cos[p] = np.cos(f)
        sin[p] = np.sin(f) * (1.0 if half == 1 else -1.0)
    return cos, sin


def causal_mask_tiles():
    m = np.zeros((128, 4, 512), np.float32)
    for j in range(4):
        for kl in range(128):
            m[kl, j, :min(128 * j + kl, 512)] = -1e30
    return m.reshape(128, 2048)



def build_program():
    nc = bass.Bass("TRN2", target_bir_lowering=False)

    dx = nc.dram_tensor("x", [NB, T, D], F32, kind="ExternalInput").ap()
    dwq = nc.dram_tensor("wq", [D, D], F32, kind="ExternalInput").ap()
    dwk = nc.dram_tensor("wk", [D, D], F32, kind="ExternalInput").ap()
    dwv = nc.dram_tensor("wv", [D, D], F32, kind="ExternalInput").ap()
    dwo = nc.dram_tensor("wo", [D, D], F32, kind="ExternalInput").ap()
    dfc1 = nc.dram_tensor("fc1", [NE, D, D], F32, kind="ExternalInput").ap()
    dfc2 = nc.dram_tensor("fc2", [NE, D, D], F32, kind="ExternalInput").ap()
    dgate = nc.dram_tensor("gate", [128, 128], F32, kind="ExternalInput").ap()
    db1 = nc.dram_tensor("b1r", [128, 128], F32, kind="ExternalInput").ap()
    db2 = nc.dram_tensor("b2", [8, D], F32, kind="ExternalInput").ap()
    dalpha = nc.dram_tensor("alphar", [128, 8], F32, kind="ExternalInput").ap()
    dlq1 = nc.dram_tensor("lq1", [1, 64], F32, kind="ExternalInput").ap()
    dlk1 = nc.dram_tensor("lk1", [1, 64], F32, kind="ExternalInput").ap()
    dlq2 = nc.dram_tensor("lq2", [1, 64], F32, kind="ExternalInput").ap()
    dlk2 = nc.dram_tensor("lk2", [1, 64], F32, kind="ExternalInput").ap()
    dlnw = nc.dram_tensor("lnw", [128, 1], F32, kind="ExternalInput").ap()
    dlnb = nc.dram_tensor("lnb", [128, 1], F32, kind="ExternalInput").ap()
    dcos = nc.dram_tensor("costab", [128, T], BF16, kind="ExternalInput").ap()
    dsin = nc.dram_tensor("sintab", [128, T], BF16, kind="ExternalInput").ap()
    dmask = nc.dram_tensor("masktab", [128, 2048], BF16, kind="ExternalInput").ap()
    dones = nc.dram_tensor("onestab", [128, 128], BF16, kind="ExternalInput").ap()
    dident = nc.dram_tensor("identtab", [128, 128], BF16, kind="ExternalInput").ap()
    dcst = nc.dram_tensor("csttab", [128, 8], F32, kind="ExternalInput").ap()
    dout = nc.dram_tensor("out", [NB, T, D], F32, kind="ExternalOutput").ap()
    swq = nc.dram_tensor("swq", [HC, 128, D], BF16, kind="Internal")
    swk = nc.dram_tensor("swk", [HC, 128, D], BF16, kind="Internal")
    swv = nc.dram_tensor("swv", [HC, 128, D], BF16, kind="Internal")
    swo = nc.dram_tensor("swo", [D, D], BF16, kind="Internal")
    sfc1 = nc.dram_tensor("sfc1", [NE, 4, 128, 8192], BF16, kind="Internal")
    sfc2 = nc.dram_tensor("sfc2", [NE, 4, 128, 8192], BF16, kind="Internal")
    sawo = nc.dram_tensor("sawo", [NB, 4, T, D], F32, kind="Internal")

    with ExitStack() as es:
        def sb(name, shape, dt=BF16):
            return es.enter_context(nc.sbuf_tensor(name, shape, dt))

        def psum(name, shape, dt=F32):
            return es.enter_context(nc.psum_tensor(name, shape, dt))

        s_cos = sb("s_cos", [128, T])
        s_sin = sb("s_sin", [128, T])
        s_mask = sb("s_mask", [128, 2048])
        s_ones = sb("s_ones", [128, 128])
        s_id = sb("s_id", [128, 128])
        s_cst = sb("s_cst", [128, 8], F32)
        s_gate = sb("s_gate", [128, 128], F32)
        s_gate_bf = sb("s_gate_bf", [128, 128])
        s_b1 = sb("s_b1", [128, 128], F32)
        s_b2bf = sb("s_b2bf", [8, D])
        s_alpha = sb("s_alpha", [128, 8], F32)
        s_lvec = sb("s_lvec", [1, 256], F32)
        s_lnw = sb("s_lnw", [128, 1], F32)
        s_lnb = sb("s_lnb", [128, 1], F32)
        s_lnwe = sb("s_lnwe", [128, 1], F32)
        s_lnbe = sb("s_lnbe", [128, 1], F32)
        s_stats = sb("s_stats", [128, 16], F32)
        s_gtmp = sb("s_gtmp", [128, 32], F32)
        s_scal = sb("s_scal", [1, 64], F32)
        s_bias = sb("s_bias", [128, 4], F32)
        s_bb = sb("s_bb", [1, 4])
        s_row = sb("s_row", [1, 2048], F32)
        s_rden = sb("s_rden", [1, 2048])
        s_cT = sb("s_cT", [8, 2048])
        s_call = sb("s_call", [128, 128], F32)
        s_cbf = sb("s_cbf", [128, 128])

        xn = sb("xn", [128, HC * T])
        ab = sb("ab", [128, 4 * T])
        yb = sb("yb", [128, 4 * T], F32)
        ybbf = yb[:].bitcast(BF16)
        az = sb("az", [128, 41 * 512])
        af = sb("af", [128, 9 * 512], F32)

        P = [psum(f"P{i}", [128, 512]) for i in range(7)]
        PT = psum("PT", [128, 1024], BF16)
        PTF = PT[:].bitcast(F32)

        p = Plan(nc, es)
        bC = Buf("consts")
        bXN = [Buf(f"xn{c}") for c in range(HC)]
        bAB = [Buf(f"ab{c}") for c in range(16)]
        bYB = [Buf(f"yb{c}") for c in range(4)]
        bAZ = [Buf(f"az{g}") for g in range(41)]
        bAF = [Buf(f"af{g}") for g in range(9)]
        bP = [Buf(f"P{i}") for i in range(7)]
        bPT = Buf("PT")
        bW = {n: Buf(n) for n in ("swq", "swk", "swv", "swo", "sfc1", "sfc2")}
        awo_toks = [[] for _ in range(NB)]
        bEV = [Buf(f"ev{j}") for j in range(4)]
        bSC = Buf("scal")
        bROW = Buf("rows")
        bRWp = [Buf("rw0"), Buf("rw1")]
        bBIAS = [Buf("bias0"), Buf("bias1")]
        bBB = [Buf("bb0"), Buf("bb1")]
        bSCp = [Buf("scal0"), Buf("scal1")]
        bST = Buf("stats")
        bGT = Buf("gatetmp")
        bLAM = Buf("lam")
        bC2 = Buf("cstuff")
        bOUT = Buf("outdma")

        def azs(g0, g1):
            return az[:, g0 * 512:g1 * 512], bAZ[g0:g1]

        def afs(g0, g1):
            return af[:, g0 * 512:g1 * 512], bAF[g0:g1]

        for dst, src in [(s_cos[:], dcos), (s_sin[:], dsin), (s_mask[:], dmask),
                         (s_ones[:], dones), (s_id[:], dident), (s_cst[:], dcst),
                         (s_gate[:], dgate), (s_b1[:], db1),
                         (s_alpha[:], dalpha), (s_lnw[:], dlnw), (s_lnb[:], dlnb)]:
            p.dma("sync", dst, src, writes=[bC])
        for i, src in enumerate([dlq1, dlk1, dlq2, dlk2]):
            p.dma("sync", s_lvec[:, 64 * i:64 * i + 64], src, writes=[bC])

        p.run("vector", lambda e: e.tensor_copy(s_gate_bf[:], s_gate[:]),
              reads=[bC], writes=[bC2])
        p.dma("gpsimd", s_b2bf[:], db2, writes=[bC2])
        p.run("vector", lambda e: e.tensor_scalar(s_lnwe[:], s_lnw[:],
                                                  float(1.0 - LAMBDA_INIT), None,
                                                  ALU.mult),
              reads=[bC], writes=[bC2])
        p.run("vector", lambda e: e.tensor_scalar(s_lnbe[:], s_lnb[:],
                                                  float(1.0 - LAMBDA_INIT), None,
                                                  ALU.mult),
              reads=[bC], writes=[bC2])
        p.run("vector", lambda e: e.tensor_mul(s_row[:, 0:64], s_lvec[:, 0:64],
                                               s_lvec[:, 64:128]),
              reads=[bC], writes=[bROW])
        p.run("vector", lambda e: e.reduce_sum(s_scal[:, 1:2], s_row[:, 0:64], axis=AX.X),
              reads=[bROW], writes=[bSC])
        p.run("vector", lambda e: e.tensor_mul(s_row[:, 0:64], s_lvec[:, 128:192],
                                               s_lvec[:, 192:256]),
              reads=[bC, bSC], writes=[bROW])
        p.run("vector", lambda e: e.reduce_sum(s_scal[:, 2:3], s_row[:, 0:64], axis=AX.X),
              reads=[bROW], writes=[bSC])
        p.run("scalar", lambda e: e.activation(s_scal[:, 3:4], s_scal[:, 1:2], AF.Exp),
              reads=[bSC], writes=[bSC])
        p.run("scalar", lambda e: e.activation(s_scal[:, 4:5], s_scal[:, 2:3], AF.Exp),
              reads=[bSC], writes=[bSC])
        p.run("vector", lambda e: e.tensor_sub(s_scal[:, 5:6], s_scal[:, 3:4],
                                               s_scal[:, 4:5]),
              reads=[bSC], writes=[bSC])
        p.run("vector", lambda e: e.tensor_scalar(s_scal[:, 0:1], s_scal[:, 5:6],
                                                  float(LAMBDA_INIT), None, ALU.add),
              reads=[bSC], writes=[bLAM])

        for w_img, w_src, buf in [(swq, dwq, bW["swq"]), (swk, dwk, bW["swk"]),
                                  (swv, dwv, bW["swv"])]:
            wv4 = w_src.rearrange("(kc p) (hc m) -> hc p kc m", p=128, m=128)
            for hcb in range(HC):
                p.dma("gpsimd",
                      w_img.ap()[hcb].rearrange("p (kc m) -> p kc m", m=128),
                      wv4[hcb], writes=[buf])
        for i in range(4):
            p.dma("gpsimd", swo.ap()[512 * i:512 * (i + 1), :],
                  dwo[512 * i:512 * (i + 1), :], writes=[bW["swo"]])
        for eidx in range(NE):
            for it in range(4):
                p.dma("gpsimd",
                      sfc1.ap()[eidx, it].rearrange("p (kc n) -> p kc n", n=512),
                      dfc1[eidx, :, 512 * it:512 * (it + 1)]
                      .rearrange("(kc p) n -> p kc n", p=128),
                      writes=[bW["sfc1"]])
                p.dma("gpsimd",
                      sfc2.ap()[eidx, it].rearrange("p (ic n) -> p ic n", n=512),
                      dfc2[eidx, :, 512 * it:512 * (it + 1)]
                      .rearrange("(ic p) n -> p ic n", p=128),
                      writes=[bW["sfc2"]])

        for b in range(NB):
            for tc in range(16):
                pr = tc % 2
                sb0 = 8 * pr
                xt, bxt = afs(4 * pr, 4 * pr + 4)
                sq, bsq = yb[:, 0:2048], [bYB[0]]
                xnt, bxnt = azs(32 + 4 * pr, 36 + 4 * pr)
                p.dma("sync", xt, dx[b, 128 * tc:128 * (tc + 1), :], writes=bxt)
                p.run("vector", lambda e, xt=xt, c=sb0: e.reduce_sum(
                    s_stats[:, c:c + 1], xt, axis=AX.X), reads=bxt, writes=[bST])
                p.run("vector", lambda e, xt=xt, sq=sq: e.tensor_mul(sq, xt, xt),
                      reads=bxt, writes=bsq)
                p.run("vector", lambda e, sq=sq, c=sb0: e.reduce_sum(
                    s_stats[:, c + 1:c + 2], sq, axis=AX.X), reads=bsq, writes=[bST])
                p.run("vector", lambda e, c=sb0: e.tensor_scalar(
                    s_stats[:, c + 2:c + 3], s_stats[:, c:c + 1], 1.0 / 2048, None,
                    ALU.mult), reads=[bST], writes=[bST])
                p.run("vector", lambda e, c=sb0: e.tensor_scalar(
                    s_stats[:, c + 3:c + 4], s_stats[:, c + 1:c + 2], 1.0 / 2048,
                    None, ALU.mult), reads=[bST], writes=[bST])
                p.run("vector", lambda e, c=sb0: e.tensor_scalar(
                    s_stats[:, c + 1:c + 2], s_stats[:, c + 2:c + 3],
                    s_stats[:, c + 2:c + 3], None, ALU.mult),
                    reads=[bST], writes=[bST])
                p.run("vector", lambda e, c=sb0: e.tensor_sub(
                    s_stats[:, c + 3:c + 4], s_stats[:, c + 3:c + 4],
                    s_stats[:, c + 1:c + 2]), reads=[bST], writes=[bST])
                p.run("scalar", lambda e, c=sb0: e.activation(
                    s_stats[:, c + 4:c + 5], s_stats[:, c + 3:c + 4], AF.Ln,
                    bias=s_cst[:, 0:1]), reads=[bST, bC], writes=[bST])
                p.run("scalar", lambda e, c=sb0: e.activation(
                    s_stats[:, c + 5:c + 6], s_stats[:, c + 4:c + 5], AF.Exp,
                    scale=-0.5), reads=[bST], writes=[bST])
                p.run("vector", lambda e, c=sb0: e.scalar_tensor_tensor(
                    s_stats[:, c + 6:c + 7], s_stats[:, c + 2:c + 3], -1.0,
                    s_stats[:, c + 5:c + 6], ALU.mult, ALU.mult),
                    reads=[bST], writes=[bST])
                p.run("scalar", lambda e, xt=xt, xnt=xnt, c=sb0: e.activation(
                    xnt, xt, AF.Identity, scale=s_stats[:, c + 5:c + 6],
                    bias=s_stats[:, c + 6:c + 7]),
                    reads=bxt + [bST], writes=bxnt)
                for half in range(2):
                    def tr(e, xnt=xnt, half=half):
                        last = None
                        for dcl in range(8):
                            dc = half * 8 + dcl
                            last = e.transpose(
                                PT[:, 128 * dcl:128 * (dcl + 1)],
                                xnt[:, 128 * dc:128 * (dc + 1)], s_id[:])
                        return last
                    p.run("tensor", tr, reads=bxnt + [bC], writes=[bPT])
                    out_ap = xn[:].rearrange("p (c n) -> p c n", c=16, n=2048)[
                        :, 8 * half:8 * half + 8, 128 * tc:128 * (tc + 1)]
                    in_ap = PT[:].rearrange("p (c n) -> p c n", c=8, n=128)
                    p.run("scalar", lambda e, out_ap=out_ap, in_ap=in_ap:
                          e.copy(out_ap, in_ap),
                          reads=[bPT], writes=[bXN[half * 8 + i] for i in range(8)])

            for tc in range(16):
                gps, bgps = (P[tc % 3], bP[tc % 3])

                def gmm(e, tc=tc, gps=gps):
                    last = None
                    for kc in range(16):
                        last = e.matmul(gps[0:128, 0:8],
                                        xn[:, 2048 * kc + 128 * tc:2048 * kc + 128 * (tc + 1)],
                                        s_gate_bf[:, 8 * kc:8 * (kc + 1)],
                                        start=(kc == 0), stop=(kc == 15))
                    return last
                p.run("tensor", gmm, reads=[bXN[i] for i in range(16)] + [bC2],
                      writes=[bgps])
                L = s_gtmp[:, 0:8]
                PEXP = s_gtmp[:, 8:16]
                MK2 = s_gtmp[:, 16:24]
                L2 = s_gtmp[:, 24:32]
                p.run("scalar", lambda e, L=L, gps=gps: e.copy(L, gps[0:128, 0:8]),
                      reads=[bgps], writes=[bGT])
                p.run("vector", lambda e, L=L: e.reduce_max(
                    s_stats[:, 0:1], L, axis=AX.X, negate=True),
                    reads=[bGT], writes=[bST])
                p.run("scalar", lambda e, L=L, PEXP=PEXP: e.activation(
                    PEXP, L, AF.Exp, bias=s_stats[:, 0:1]),
                    reads=[bGT, bST], writes=[bGT])
                p.run("vector", lambda e: e.tensor_scalar(
                    s_stats[:, 1:2], s_stats[:, 0:1], -1.0, None, ALU.mult),
                    reads=[bST], writes=[bST])
                p.run("vector", lambda e, L=L, L2=L2: e.tensor_scalar(
                    L2, L, s_stats[:, 1:2], None, ALU.is_ge),
                    reads=[bGT, bST], writes=[bGT])
                p.run("vector", lambda e, L=L, L2=L2: e.scalar_tensor_tensor(
                    L2, L2, -1e30, L, ALU.mult, ALU.add),
                    reads=[bGT], writes=[bGT])
                p.run("vector", lambda e, L2=L2: e.reduce_max(
                    s_stats[:, 2:3], L2, axis=AX.X),
                    reads=[bGT], writes=[bST])
                p.run("vector", lambda e, L=L, MK2=MK2: e.tensor_scalar(
                    MK2, L, s_stats[:, 2:3], None, ALU.is_ge),
                    reads=[bGT, bST], writes=[bGT])
                p.run("vector", lambda e, PEXP=PEXP, MK2=MK2: e.tensor_mul(
                    PEXP, PEXP, MK2), reads=[bGT], writes=[bGT])
                p.run("vector", lambda e, PEXP=PEXP: e.reduce_sum(
                    s_stats[:, 3:4], PEXP, axis=AX.X),
                    reads=[bGT], writes=[bST])
                p.run("vector", lambda e: e.reciprocal(
                    s_stats[:, 4:5], s_stats[:, 3:4]), reads=[bST], writes=[bST])
                p.run("vector", lambda e, PEXP=PEXP, tc=tc: e.scalar_tensor_tensor(
                    s_call[:, 8 * tc:8 * (tc + 1)], PEXP, s_stats[:, 4:5],
                    s_alpha[:, 0:8], ALU.mult, ALU.mult),
                    reads=[bGT, bST, bC], writes=[bC2])
            p.run("vector", lambda e: e.tensor_copy(s_cbf[:], s_call[:]),
                  reads=[bC2], writes=[bC2])
            for tc in range(16):
                p.run("tensor", lambda e, tc=tc: e.transpose(
                    PT[0:8, 0:128],
                    s_cbf[:, 8 * tc:8 * (tc + 1)], s_id[:]),
                    reads=[bC2, bC], writes=[bPT])
                p.run("scalar", lambda e, tc=tc: e.copy(
                    s_cT[:, 128 * tc:128 * (tc + 1)], PT[0:8, 0:128]),
                    reads=[bPT], writes=[bC2])

            WCB = [azs(4 * i, 4 * i + 4) for i in range(2)]
            QCH = [azs(8, 12), azs(12, 16)]
            KCH = [azs(16, 20), azs(20, 24)]
            VCH = [azs(24, 28), azs(28, 32)]
            SQS = azs(32, 36)
            RTM = [azs(32, 33), azs(33, 34), azs(34, 35)]
            PBS = [azs(36 + i, 37 + i) for i in range(4)]
            SQD = azs(40, 41)
            ROT = [(P[0], bP[0]), (P[1], bP[1]), (P[2], bP[2])]
            rr = {"n": 0}

            def nrot():
                ps, bps = ROT[rr["n"] % 3]
                rr["n"] += 1
                return ps, bps

            rq = {"n": 0}

            def nrot7():
                ps, bps = (P[rq["n"] % 7], bP[rq["n"] % 7])
                rq["n"] += 1
                return ps, bps

            gi = {"t": 0}
            defq = []
            DEFER = 20

            def tick(w=1):
                gi["t"] += w

            def pump(force=False):
                while defq and (force or defq[0][0] <= gi["t"]):
                    defq.pop(0)[1]()

            shmask = list(range(16, 32)) + list(range(16))

            def qkv_head(h, par):
                hb = h % 2
                qc, bqc = QCH[par]
                kc_, bkc = KCH[par]
                vc, bvc = VCH[par]
                bxnall = [bXN[i] for i in range(16)]
                wq_s, bwq = WCB[0]
                wk_s, bwk = WCB[1]
                p.dma("sync", wq_s, swq.ap()[h], reads=[bW["swq"]], writes=bwq)
                p.dma("sync", wk_s, swk.ap()[h], reads=[bW["swk"]], writes=bwk)
                for (wsl, bws, dst, bdst) in [(wq_s, bwq, qc, bqc),
                                              (wk_s, bwk, kc_, bkc)]:
                    for tt in range(4):
                        ps, bps = nrot7()

                        def mm(e, wsl=wsl, tt=tt, ps=ps):
                            last = None
                            for kk in range(16):
                                last = e.matmul(
                                    ps[:], wsl[:, 128 * kk:128 * (kk + 1)],
                                    xn[:, 2048 * kk + 512 * tt:2048 * kk + 512 * (tt + 1)],
                                    start=(kk == 0), stop=(kk == 15))
                            return last
                        p.run("tensor", mm, reads=bws + bxnall, writes=[bps])
                        tick(2)
                        pump()
                        raw, braw = RTM[0]
                        rot, brot = RTM[1]
                        t1, bt1 = RTM[2]
                        p.run("scalar", lambda e, raw=raw, ps=ps: e.copy(
                            raw[:, 0:512], ps[:]), reads=[bps], writes=braw)
                        p.run("vector", lambda e, raw=raw, rot=rot: e.stream_shuffle(
                            rot[:, 0:512], raw[:, 0:512], shmask),
                            reads=braw, writes=brot)
                        p.run("vector", lambda e, raw=raw, t1=t1, tt=tt: e.tensor_mul(
                            t1[:, 0:512], raw[:, 0:512],
                            s_cos[:, 512 * tt:512 * (tt + 1)]),
                            reads=braw + [bC], writes=bt1)
                        p.run("vector", lambda e, rot=rot, tt=tt: e.tensor_mul(
                            rot[:, 0:512], rot[:, 0:512],
                            s_sin[:, 512 * tt:512 * (tt + 1)]),
                            reads=brot + [bC], writes=brot)
                        p.run("vector", lambda e, dst=dst, t1=t1, rot=rot, tt=tt:
                              e.tensor_add(dst[:, 512 * tt:512 * (tt + 1)],
                                           t1[:, 0:512], rot[:, 0:512]),
                              reads=bt1 + brot, writes=bdst)
                wv_s, bwv = WCB[0]
                p.dma("sync", wv_s, swv.ap()[h], reads=[bW["swv"]], writes=bwv)
                for tcv in range(16):
                    ps, bps = nrot7()

                    def vmm(e, wv_s=wv_s, tcv=tcv, ps=ps):
                        last = None
                        for kk in range(16):
                            last = e.matmul(
                                ps[:, 0:128],
                                xn[:, 2048 * kk + 128 * tcv:2048 * kk + 128 * (tcv + 1)],
                                wv_s[:, 128 * kk:128 * (kk + 1)],
                                start=(kk == 0), stop=(kk == 15))
                        return last
                    p.run("tensor", vmm, reads=bwv + bxnall, writes=[bps])
                    tick(1)
                    pump()
                    p.run("scalar", lambda e, vc=vc, tcv=tcv, ps=ps: e.copy(
                        vc[:, 128 * tcv:128 * (tcv + 1)], ps[:, 0:128]),
                        reads=[bps], writes=bvc)
                sqs, bsqs = SQS
                sc0 = 32 * hb
                for (srctile, bsrc, cb) in [(qc, bqc, sc0 + 8), (kc_, bkc, sc0 + 16)]:
                    p.run("vector", lambda e, s=srctile, sqs=sqs: e.tensor_mul(
                        sqs, s, s), reads=bsrc, writes=bsqs)
                    for a in range(2):
                        for i4 in range(4):
                            ps, bps = nrot7()
                            p.run("tensor", lambda e, a=a, i4=i4, sqs=sqs, ps=ps:
                                  e.matmul(ps[0:1, 0:512],
                                           s_ones[64 * a:64 * (a + 1), 0:1],
                                           sqs[64 * a:64 * (a + 1),
                                               512 * i4:512 * (i4 + 1)],
                                           start=True, stop=True),
                                  reads=bsqs + [bC], writes=[bps])
                            p.run("vector", lambda e, ps=ps, c=cb + 4 * a + i4:
                                  e.reduce_max(s_scal[:, c:c + 1], ps[0:1, 0:512],
                                               axis=AX.X),
                                  reads=[bps], writes=[bSCp[hb]])
                for a in range(2):
                    q0 = sc0 + 8 + 4 * a
                    k0 = sc0 + 16 + 4 * a
                    t0 = sc0 + 24
                    p.run("vector", lambda e, q0=q0, t0=t0: e.reduce_max(
                        s_scal[:, t0:t0 + 1], s_scal[:, q0:q0 + 4], axis=AX.X),
                        reads=[bSCp[hb]], writes=[bSCp[hb]])
                    p.run("vector", lambda e, k0=k0, t0=t0: e.reduce_max(
                        s_scal[:, t0 + 1:t0 + 2], s_scal[:, k0:k0 + 4], axis=AX.X),
                        reads=[bSCp[hb]], writes=[bSCp[hb]])
                    p.run("vector", lambda e, t0=t0: e.tensor_mul(
                        s_scal[:, t0 + 2:t0 + 3], s_scal[:, t0:t0 + 1],
                        s_scal[:, t0 + 1:t0 + 2]), reads=[bSCp[hb]], writes=[bSCp[hb]])
                    p.run("scalar", lambda e, t0=t0: e.activation(
                        s_scal[:, t0 + 3:t0 + 4], s_scal[:, t0 + 2:t0 + 3], AF.Ln),
                        reads=[bSCp[hb]], writes=[bSCp[hb]])
                    p.run("scalar", lambda e, t0=t0: e.activation(
                        s_scal[:, t0 + 4:t0 + 5], s_scal[:, t0 + 3:t0 + 4], AF.Exp,
                        scale=0.5), reads=[bSCp[hb]], writes=[bSCp[hb]])
                    p.run("vector", lambda e, t0=t0: e.tensor_scalar(
                        s_scal[:, t0 + 5:t0 + 6], s_scal[:, t0 + 4:t0 + 5], -0.125,
                        None, ALU.mult), reads=[bSCp[hb]], writes=[bSCp[hb]])
                    p.run("vector", lambda e, t0=t0, c=2 * hb + a: e.tensor_copy(
                        s_bb[0:1, c:c + 1], s_scal[:, t0 + 5:t0 + 6]),
                        reads=[bSCp[hb]], writes=[bBB[hb]])
                    ps, bps = nrot7()
                    p.run("tensor", lambda e, c=2 * hb + a, ps=ps: e.matmul(
                        ps[:, 0:1], s_ones[0:1, :], s_bb[0:1, c:c + 1],
                        start=True, stop=True), reads=[bBB[hb], bC], writes=[bps])
                    p.run("scalar", lambda e, c=2 * hb + a, ps=ps: e.copy(
                        s_bias[:, c:c + 1], ps[:, 0:1]), reads=[bps],
                        writes=[bBIAS[hb]])

            def attn_head(h, par, hslot):
                qc, bqc = QCH[par]
                kc_, bkc = KCH[par]
                vc, bvc = VCH[par]
                hb = h % 2
                LAG = 3
                insts = [(qt, kk, a) for qt in range(4)
                         for kk in range(4 * qt + 4) for a in range(2)]
                lastof = {}
                for i, (qt, kk, a) in enumerate(insts):
                    lastof[qt] = i
                sbank = {}

                def emit_dpv(idx):
                    qt, kk, a = insts[idx]
                    nkc = 4 * qt + 4
                    pu, bpu = (P[3], bP[3]) if a == 0 else (P[4], bP[4])
                    pd, bpd = (P[5], bP[5]) if a == 0 else (P[6], bP[6])
                    pbuf, bpb = PBS[idx % 4]

                    def dpv(e, kk=kk, nkc=nkc, pu=pu, pd=pd, pbuf=pbuf, vc=vc):
                        e.matmul(pd[0:1, :], s_ones[:, 0:1], pbuf[:, 0:512],
                                 start=(kk == 0), stop=(kk == nkc - 1))
                        return e.matmul(pu[:], vc[:, 128 * kk:128 * (kk + 1)],
                                        pbuf[:, 0:512],
                                        start=(kk == 0), stop=(kk == nkc - 1))
                    p.run("tensor", dpv, reads=bpb + bvc + [bC],
                          writes=[bpu, bpd])

                def emit_combine_a(qt):
                    p_ = qt % 2
                    r0 = 1024 * p_
                    U1C, bU1C = afs(5 + 2 * p_, 6 + 2 * p_)
                    U2C, bU2C = afs(6 + 2 * p_, 7 + 2 * p_)
                    p.run("vector", lambda e, r0=r0: e.reciprocal(
                        s_row[:, r0:r0 + 512], P[5][0:1, :]),
                        reads=[bP[5]], writes=[bRWp[p_]])
                    p.run("vector", lambda e, r0=r0: e.reciprocal(
                        s_row[:, r0 + 512:r0 + 1024], P[6][0:1, :]),
                        reads=[bP[6]], writes=[bRWp[p_]])
                    p.run("vector", lambda e, U1C=U1C: e.tensor_copy(
                        U1C, P[3][:]), reads=[bP[3]], writes=bU1C)
                    p.run("vector", lambda e, U2C=U2C: e.tensor_copy(
                        U2C, P[4][:]), reads=[bP[4]], writes=bU2C)

                def cb_part1(qt):
                    p_ = qt % 2
                    r0 = 1024 * p_
                    RR1, bRR1 = afs(2, 3)
                    p.run("vector", lambda e, r0=r0: e.tensor_copy(
                        s_rden[0:1, r0:r0 + 512], s_row[:, r0:r0 + 512]),
                        reads=[bRWp[p_]], writes=[bRWp[p_]])
                    p.run("vector", lambda e, r0=r0: e.tensor_scalar(
                        s_rden[0:1, r0 + 512:r0 + 1024],
                        s_row[:, r0 + 512:r0 + 1024],
                        s_scal[:, 0:1], None, ALU.mult),
                        reads=[bRWp[p_], bLAM], writes=[bRWp[p_]])
                    p.run("tensor", lambda e, r0=r0: e.matmul(
                        PTF, s_ones[0:1, :], s_rden[0:1, r0:r0 + 512],
                        start=True, stop=True), reads=[bRWp[p_], bC], writes=[bPT])
                    p.run("scalar", lambda e, RR1=RR1: e.copy(RR1, PTF),
                          reads=[bPT], writes=bRR1)

                def cb_part2(qt):
                    p_ = qt % 2
                    r0 = 1024 * p_
                    U1C, bU1C = afs(5 + 2 * p_, 6 + 2 * p_)
                    U2C, bU2C = afs(6 + 2 * p_, 7 + 2 * p_)
                    DIF, bDIF = afs(0, 1)
                    A2S, bA2S = afs(1, 2)
                    RR1, bRR1 = afs(2, 3)
                    RR2, bRR2 = afs(3, 4)
                    sqd, bsqd = SQD
                    p.run("tensor", lambda e, r0=r0: e.matmul(
                        PTF, s_ones[0:1, :], s_rden[0:1, r0 + 512:r0 + 1024],
                        start=True, stop=True), reads=[bRWp[p_], bC], writes=[bPT])
                    p.run("scalar", lambda e, RR2=RR2: e.copy(RR2, PTF),
                          reads=[bPT], writes=bRR2)
                    p.run("vector", lambda e, DIF=DIF, RR1=RR1, U1C=U1C:
                          e.tensor_mul(DIF, U1C, RR1),
                          reads=bU1C + bRR1, writes=bDIF)
                    p.run("vector", lambda e, A2S=A2S, RR2=RR2, U2C=U2C:
                          e.tensor_mul(A2S, U2C, RR2),
                          reads=bU2C + bRR2, writes=bA2S)
                    p.run("vector", lambda e, DIF=DIF, A2S=A2S:
                          e.tensor_sub(DIF, DIF, A2S),
                          reads=bA2S + bDIF, writes=bDIF)
                    p.run("vector", lambda e, DIF=DIF, sqd=sqd: e.tensor_mul(
                        sqd[:, 0:512], DIF, DIF), reads=bDIF, writes=bsqd)

                def cb_part3(qt):
                    p_ = qt % 2
                    r0 = 1024 * p_
                    sqd, bsqd = SQD
                    p.run("tensor", lambda e, sqd=sqd: e.matmul(
                        PTF[0:1, 0:512], s_ones[:, 0:1], sqd[:, 0:512],
                        start=True, stop=True), reads=bsqd + [bC], writes=[bPT])
                    p.run("scalar", lambda e, r0=r0: e.activation(
                        s_row[:, r0:r0 + 512], PTF[0:1, 0:512], AF.Ln,
                        scale=1.0 / 128, bias=s_cst[0:1, 1:2]),
                        reads=[bPT, bC], writes=[bRWp[p_]])
                    p.run("scalar", lambda e, r0=r0: e.activation(
                        s_row[:, r0:r0 + 512], s_row[:, r0:r0 + 512], AF.Exp,
                        scale=-0.5), reads=[bRWp[p_]], writes=[bRWp[p_]])
                    p.run("vector", lambda e, r0=r0: e.tensor_copy(
                        s_rden[0:1, r0:r0 + 512], s_row[:, r0:r0 + 512]),
                        reads=[bRWp[p_]], writes=[bRWp[p_]])

                def cb_part4(qt, hslot=hslot):
                    p_ = qt % 2
                    r0 = 1024 * p_
                    DIF, bDIF = afs(0, 1)
                    RMR, bRMR = afs(4, 5)
                    p.run("tensor", lambda e, r0=r0: e.matmul(
                        PTF, s_ones[0:1, :], s_rden[0:1, r0:r0 + 512],
                        start=True, stop=True), reads=[bRWp[p_], bC], writes=[bPT])
                    p.run("scalar", lambda e, RMR=RMR: e.copy(RMR, PTF),
                          reads=[bPT], writes=bRMR)
                    p.run("vector", lambda e, DIF=DIF, RMR=RMR: e.tensor_mul(
                        DIF, DIF, RMR), reads=bDIF + bRMR, writes=bDIF)
                    p.run("scalar", lambda e, DIF=DIF, qt=qt:
                          e.activation(ab[:, 2048 * hslot + 512 * qt:
                                          2048 * hslot + 512 * (qt + 1)],
                                       DIF, AF.Identity, scale=s_lnwe[:],
                                       bias=s_lnbe[:]),
                          reads=bDIF + [bC2], writes=[bAB[4 * hslot + qt]])

                def post_s(i):
                    if LAG <= i < len(insts) + LAG:
                        emit_dpv(i - LAG)
                        pqt = insts[i - LAG][0]
                        if i - LAG == lastof[pqt]:
                            emit_combine_a(pqt)
                            cb_part1(pqt)
                            cb_part2(pqt)
                            cb_part3(pqt)
                            cb_part4(pqt)

                for i, (qt, kk, a) in enumerate(insts):
                    j = kk - 4 * qt
                    ps, bps = nrot()
                    sbank[i] = (ps, bps)

                    def sg(e, a=a, kk=kk, qt=qt, j=j, ps=ps, kc_=kc_, qc=qc):
                        last = e.matmul(ps[:], kc_[64 * a:64 * (a + 1),
                                                   128 * kk:128 * (kk + 1)],
                                        qc[64 * a:64 * (a + 1),
                                           512 * qt:512 * (qt + 1)],
                                        start=True, stop=(j < 0))
                        if j >= 0:
                            last = e.matmul(ps[:], s_id[:],
                                            s_mask[:, 512 * j:512 * (j + 1)],
                                            start=False, stop=True)
                        return last
                    p.run("tensor", sg, reads=bkc + bqc + [bC], writes=[bps])
                    pbuf, bpb = PBS[i % 4]
                    p.run("scalar", lambda e, ps=ps, pbuf=pbuf, c=2 * hb + a:
                          e.activation(pbuf[:, 0:512], ps[:], AF.Exp,
                                       scale=0.125, bias=s_bias[:, c:c + 1]),
                          reads=[bps, bBIAS[hb]], writes=bpb)
                    post_s(i)
                    tick(1)
                    pump()
                n = len(insts)
                for i in range(n, n + LAG + 1):
                    post_s(i)

            WOS = ybbf[:, 0:8192]

            def wo_load(hp):
                for i in range(4):
                    p.dma("sync", WOS[:, 2048 * i:2048 * (i + 1)],
                          swo.ap()[128 * (4 * hp + i):128 * (4 * hp + i + 1), :],
                          reads=[bW["swo"]], writes=[bYB[i // 2]])

            def wo_pass(hp):
                for qc in range(16):
                    if defq:
                        defq.pop(0)[1]()
                    for dt in range(4):
                        ppar = (4 * qc + dt) % 4
                        ps, bps = (P[ppar], bP[ppar])

                        def womm(e, qc=qc, dt=dt, ps=ps):
                            last = None
                            for i in range(4):
                                last = e.matmul(
                                    ps[:], ab[:, 2048 * i + 128 * qc:
                                              2048 * i + 128 * (qc + 1)],
                                    WOS[:, 2048 * i + 512 * dt:
                                        2048 * i + 512 * (dt + 1)],
                                    start=(i == 0), stop=(i == 3))
                            return last
                        p.run("tensor", womm,
                              reads=[bAB[4 * i + qc // 4] for i in range(4)]
                              + [bYB[0], bYB[1]],
                              writes=[bps])
                        if qc == 0 and dt == 0:
                            assert len(defq) <= 7, len(defq)
                        ej = (4 * qc + dt) % 4
                        EV = yb[:, 4096 + 512 * ej:4608 + 512 * ej]
                        p.run("vector", lambda e, EV=EV, ps=ps: e.tensor_copy(
                            EV, ps[:]), reads=[bps], writes=[bEV[ej], bYB[2]])
                        awo_toks[b].append(p.dma(
                            "gpsimd",
                            sawo.ap()[b, hp, 128 * qc:128 * (qc + 1),
                                      512 * dt:512 * (dt + 1)],
                            EV, reads=[bEV[ej]]))
                if hp < 3:
                    wo_load(hp + 1)

            for hp in range(4):
                heads = list(range(4 * hp, 4 * hp + 4))
                if hp == 0:
                    wo_load(0)
                    qkv_head(heads[0], 0)
                for i, h in enumerate(heads):
                    if i + 1 < len(heads):
                        qkv_head(heads[i + 1], (i + 1) % 2)
                    elif hp < 3:
                        qkv_head(4 * hp + 4, 0)
                    attn_head(h, i % 2, h - 4 * hp)
                wo_pass(hp)

            FCR = [azs(16 * i, 16 * (i + 1)) for i in range(2)]
            STG = [afs(0, 4), afs(4, 8)]

            def emit_b2(tt):
                for tc in range(4):
                    gtc = 4 * tt + tc
                    ysl = yb[:, 2048 * tc:2048 * (tc + 1)]
                    for ht in range(4):
                        bi = [2, 5, 6][(4 * tc + ht) % 3]
                        p.run("tensor", lambda e, gtc=gtc, ht=ht, bi=bi: e.matmul(
                            P[bi][:], s_cT[:, 128 * gtc:128 * (gtc + 1)],
                            s_b2bf[:, 512 * ht:512 * (ht + 1)],
                            start=True, stop=True),
                            reads=[bC2], writes=[bP[bi]])
                        ywr = [bYB[tc]] + (bEV if (tt == 0 and tc == 2) else [])
                        p.run("vector", lambda e, ysl=ysl, ht=ht, bi=bi:
                              e.tensor_copy(ysl[:, 512 * ht:512 * (ht + 1)],
                                            P[bi][:]),
                              reads=[bP[bi]], writes=ywr)

            emit_b2(0)
            for tt in range(4):
                sct = {"n": 0}
                for tc in range(4):
                    gtc = 4 * tt + tc
                    ysl = yb[:, 2048 * tc:2048 * (tc + 1)]
                    srcs = [dx[b, 128 * gtc:128 * (gtc + 1), :]] + [
                        sawo.ap()[b, hp, 128 * gtc:128 * (gtc + 1), :]
                        for hp in range(4)]
                    for src in srcs:
                        stg, bstg = STG[sct["n"] % 2]
                        sct["n"] += 1
                        p.dma("gpsimd", stg[:, 0:2048], src,
                              writes=bstg, deps=awo_toks[b])
                        p.run("vector", lambda e, ysl=ysl, stg=stg: e.tensor_add(
                            ysl, ysl, stg[:, 0:2048]),
                            reads=bstg + [bYB[tc]], writes=[bYB[tc]])
                for eidx in range(NE):
                    hsl = ab[:]
                    bhsl = list(bAB)
                    for it in range(4):
                        ring, bring = FCR[(eidx * 8 + it) % 2]
                        p.dma("sync", ring, sfc1.ap()[eidx, it],
                              reads=[bW["sfc1"]], writes=bring)
                        for lic in range(4):
                            ic = 4 * it + lic
                            ppar = ic % 2
                            ps, bps = (P[ppar], bP[ppar])

                            def f1mm(e, ring=ring, lic=lic, tt=tt, ps=ps):
                                last = None
                                for kk in range(16):
                                    last = e.matmul(
                                        ps[:],
                                        ring[:, 512 * kk + 128 * lic:
                                             512 * kk + 128 * (lic + 1)],
                                        xn[:, 2048 * kk + 512 * tt:
                                           2048 * kk + 512 * (tt + 1)],
                                        start=(kk == 0), stop=(kk == 15))
                                return last
                            p.run("tensor", f1mm,
                                  reads=bring + [bXN[i] for i in range(16)],
                                  writes=[bps])
                            p.run("scalar", lambda e, hsl=hsl, ic=ic, ps=ps,
                                  eidx=eidx: e.activation(
                                      hsl[:, 512 * ic:512 * (ic + 1)], ps[:],
                                      AF.Gelu, bias=s_b1[:, 16 * eidx + ic:
                                                         16 * eidx + ic + 1]),
                                  reads=[bps, bC], writes=[bAB[ic]])
                    for ht in range(4):
                        ring, bring = FCR[(eidx * 8 + 4 + ht) % 2]
                        p.dma("sync", ring, sfc2.ap()[eidx, ht],
                              reads=[bW["sfc2"]], writes=bring)
                        for tc in range(4):
                            gtc = 4 * tt + tc
                            ppar = (4 * ht + tc) % 2
                            ps, bps = (P[3 + ppar], bP[3 + ppar])

                            def f2mm(e, ring=ring, hsl=hsl, tc=tc, ps=ps):
                                last = None
                                for ic in range(16):
                                    last = e.matmul(
                                        ps[:],
                                        hsl[:, 512 * ic + 128 * tc:
                                            512 * ic + 128 * (tc + 1)],
                                        ring[:, 512 * ic:512 * (ic + 1)],
                                        start=(ic == 0), stop=(ic == 15))
                                return last
                            p.run("tensor", f2mm, reads=bring + bhsl,
                                  writes=[bps])
                            ysl = yb[:, 2048 * tc + 512 * ht:
                                     2048 * tc + 512 * (ht + 1)]
                            p.run("vector", lambda e, ysl=ysl, ps=ps, gtc=gtc,
                                  eidx=eidx: e.scalar_tensor_tensor(
                                      ysl, ps[:],
                                      s_call[:, 8 * gtc + eidx:8 * gtc + eidx + 1],
                                      ysl, ALU.mult, ALU.add),
                                  reads=[bps, bC2, bYB[tc]], writes=[bYB[tc]])
                for tc in range(4):
                    gtc = 4 * tt + tc
                    p.dma("scalar", dout[b, 128 * gtc:128 * (gtc + 1), :],
                          yb[:, 2048 * tc:2048 * (tc + 1)],
                          reads=[bYB[tc]])
                if tt < 3:
                    emit_b2(tt + 1)

        p.waiter("scalar", [(s, s.count) for s in p.dsems["scalar"]])

        with nc.allow_low_precision(reason="bf16 matmul operands by design"):
            with nc.Block() as blk:
                p.emit(blk)
    return nc



_NC_CACHE = {}


def kernel(L0_attn_x, L0_attn_Wq, L0_attn_Wk, L0_attn_Wv, L0_attn_Wo,
           L0_attn_lambda_q1, L0_attn_lambda_k1, L0_attn_lambda_q2,
           L0_attn_lambda_k2, L0_attn_ln_w, L0_attn_ln_b, L0_ffn_gate_w,
           L0_ffn_fc1_w, L0_ffn_fc1_b, L0_ffn_fc2_w, L0_ffn_fc2_b,
           L0_ffn_alpha):
    x = np.ascontiguousarray(np.asarray(L0_attn_x, dtype=np.float32))
    perm = rope_perm_cols()
    wq = np.ascontiguousarray(np.asarray(L0_attn_Wq, np.float32)[:, perm])
    wk = np.ascontiguousarray(np.asarray(L0_attn_Wk, np.float32)[:, perm])
    wv = np.ascontiguousarray(np.asarray(L0_attn_Wv, np.float32))
    wo = np.ascontiguousarray(np.asarray(L0_attn_Wo, np.float32))
    fc1 = np.ascontiguousarray(np.asarray(L0_ffn_fc1_w, np.float32))
    fc2 = np.ascontiguousarray(np.asarray(L0_ffn_fc2_w, np.float32))
    gate = np.ascontiguousarray(
        np.asarray(L0_ffn_gate_w, np.float32).reshape(16, 128, 8)
        .transpose(1, 0, 2).reshape(128, 128))
    b1r = np.ascontiguousarray(
        np.asarray(L0_ffn_fc1_b, np.float32).reshape(8, 16, 128)
        .transpose(2, 0, 1).reshape(128, 128))
    b2 = np.ascontiguousarray(np.asarray(L0_ffn_fc2_b, np.float32))
    alphar = np.ascontiguousarray(
        np.tile(np.asarray(L0_ffn_alpha, np.float32)[None, :], (128, 1)))
    cos_t, sin_t = rope_tables()
    consts = {
        "costab": cos_t.astype(bf), "sintab": sin_t.astype(bf),
        "masktab": causal_mask_tiles().astype(bf),
        "onestab": np.ones((128, 128), np.float32).astype(bf),
        "identtab": np.eye(128, dtype=np.float32).astype(bf),
        "csttab": np.tile(np.array([[1e-5, 1e-8, 0, 0, 0, 0, 0, 0]],
                                   np.float32), (128, 1)),
    }
    smalls = {
        "wq": wq, "wk": wk, "wv": wv, "wo": wo, "fc1": fc1, "fc2": fc2,
        "gate": gate, "b1r": b1r, "b2": b2, "alphar": alphar,
        "lq1": np.asarray(L0_attn_lambda_q1, np.float32).reshape(1, 64),
        "lk1": np.asarray(L0_attn_lambda_k1, np.float32).reshape(1, 64),
        "lq2": np.asarray(L0_attn_lambda_q2, np.float32).reshape(1, 64),
        "lk2": np.asarray(L0_attn_lambda_k2, np.float32).reshape(1, 64),
        "lnw": np.asarray(L0_attn_ln_w, np.float32).reshape(128, 1),
        "lnb": np.asarray(L0_attn_ln_b, np.float32).reshape(128, 1),
    }
    in_maps = []
    for core in range(8):
        m = {"x": np.ascontiguousarray(x[2 * core:2 * core + 2])}
        m.update(smalls)
        m.update(consts)
        in_maps.append(m)
    if "nc" not in _NC_CACHE:
        _NC_CACHE["nc"] = build_program()
    nc = _NC_CACHE["nc"]
    res = run_bass_kernel_spmd(nc, in_maps, core_ids=list(range(8)))
    out = np.concatenate([np.asarray(r["out"], np.float32)
                          for r in res.results], axis=0)
    return out
